# version 14
# speedup vs baseline: 1.7331x; 1.0052x over previous
import numpy as np
import concourse.bass as bass
import concourse.mybir as mybir
from concourse.bass_utils import run_bass_kernel_spmd

F32 = mybir.dt.float32
BF16 = mybir.dt.bfloat16
AF = mybir.ActivationFunctionType
ALU = mybir.AluOpType

D = 2048
T = 1024
NFT = 16
C = 128
NCH = 8
NEG = -1e30
INTER = 5632
NIT = 44

EMIT_ATTN_A = True
EMIT_ATTN_B = True
EMIT_FFN = True
FFN_STAGE = 3


CC_INSTS = []
DBG_P1 = False


class Tok:
    __slots__ = ("sem", "val")

    def __init__(self, sem, val):
        self.sem = sem
        self.val = val


class RecRes:
    __slots__ = ("ent",)

    def __init__(self, ent):
        self.ent = ent

    def then_inc(self, sem, amt):
        self.ent[3].append((sem, amt))
        return self


class RecEng:
    __slots__ = ("log",)

    def __init__(self, log):
        self.log = log

    def __getattr__(self, name):
        log = self.log

        def call(*a, **kw):
            ent = [name, a, kw, [], None]
            log.append(ent)
            return RecRes(ent)

        return call


class Ker:
    ENGS = ("tensor", "vector", "scalar", "gpsimd", "sync")

    def __init__(self, nc, block, sems):
        self.nc = nc
        self.block = block
        self.st = {}
        for name in self.ENGS:
            self.st[name] = dict(sem=sems[name], cnt=0, amt=1)
        self.dma_pool = sems["dma_pool"]
        self.dma_ch = {}
        self.last_wait = {e: {} for e in self.ENGS}
        self.logs = {e: [] for e in self.ENGS}

    def _ch(self, name):
        if name not in self.dma_ch:
            idx = len(self.dma_ch)
            assert idx < len(self.dma_pool), "out of DMA channel semaphores"
            self.dma_ch[name] = dict(sem=self.dma_pool[idx], cnt=0)
        return self.dma_ch[name]

    def _deps(self, eng_name, deps):
        lw = self.last_wait[eng_name]
        log = self.logs[eng_name]
        for t in deps:
            if t is None:
                continue
            key = t.sem.num if hasattr(t.sem, "num") else id(t.sem)
            if lw.get(key, -1) < t.val:
                log.append(["wait_ge", (t.sem, t.val), {}, [], None])
                lw[key] = t.val

    def _emit(self, stream, eng_name, f, deps, inc=True):
        s = self.st[stream]
        self._deps(eng_name, deps)
        ins = f(RecEng(self.logs[eng_name]))
        if inc:
            ins.then_inc(s["sem"], s["amt"])
            s["cnt"] += s["amt"]
        return Tok(s["sem"], s["cnt"])

    def t(self, f, deps=()):
        return self._emit("tensor", "tensor", f, deps)

    def v(self, f, deps=()):
        return self._emit("vector", "vector", f, deps)

    def s(self, f, deps=()):
        return self._emit("scalar", "scalar", f, deps)

    def g(self, f, deps=()):
        return self._emit("gpsimd", "gpsimd", f, deps)

    def dma(self, q, out, in_, deps=(), ch="x"):
        eng_name = "sync" if q == "dS" else "gpsimd"
        c = self._ch(q + ":" + ch)
        if c["cnt"] > 0:
            self._deps(eng_name, [Tok(c["sem"], c["cnt"])])
        self._deps(eng_name, deps)
        self.logs[eng_name].append(
            ["dma_start", (), dict(out=out, in_=in_), [(c["sem"], 16)], None])
        c["cnt"] += 16
        return Tok(c["sem"], c["cnt"])

    def waiter(self, eng_name, deps):
        self._deps(eng_name, deps)

    def collective(self, kind, op, ins, outs, groups, sem, deps=()):
        s = self.st["gpsimd"]
        self._deps("gpsimd", deps)

        def post(bi):
            CC_INSTS.append(bi.ins if hasattr(bi, "ins") else bi)

        self.logs["gpsimd"].append(
            ["collective_compute", (kind, op),
             dict(ins=ins, outs=outs, replica_groups=groups),
             [(s["sem"], s["amt"])], post])
        s["cnt"] += s["amt"]
        return Tok(s["sem"], s["cnt"])

    def flush(self):
        for eng_name in self.ENGS:
            entries = self.logs[eng_name]
            if not entries:
                continue

            def body(eng, entries=entries):
                for method, a, kw, incs, post in entries:
                    r = getattr(eng, method)(*a, **kw)
                    for sem, amt in incs:
                        r.then_inc(sem, amt)
                    if post is not None:
                        post(r)

            getattr(self.block, eng_name)(body)
            self.logs[eng_name] = []


def _hc(a):
    return np.ascontiguousarray(a)


def host_prep(inputs):
    inp = {k: np.asarray(v) for k, v in inputs.items()}
    x = inp["L0_attn_a_x"]
    shared = {}

    shared["aWqkv"] = _hc(np.concatenate(
        [inp["L0_attn_a_Wq"], inp["L0_attn_a_Wk"], inp["L0_attn_a_Wv"]], axis=1))
    awab = np.zeros((2048, 64), np.float32)
    awab[:, 0:16] = inp["L0_attn_a_Wa"]
    awab[:, 32:48] = inp["L0_attn_a_Wb"]
    shared["aWab"] = _hc(awab)
    shared["aWo"] = _hc(inp["L0_attn_a_Wo"])
    acw = np.concatenate([inp["L0_attn_a_qcw"], inp["L0_attn_a_kcw"],
                          inp["L0_attn_a_vcw"]], axis=0)
    shared["aconvw"] = _hc(acw.reshape(64, 128, 4).transpose(1, 0, 2).reshape(128, 256))
    acb = np.concatenate([inp["L0_attn_a_qcb"], inp["L0_attn_a_kcb"],
                          inp["L0_attn_a_vcb"]], axis=0)
    shared["aconvb"] = _hc(acb.reshape(64, 128).T)

    Wq = inp["L0_attn_b_W_qkvz"].reshape(2048, 16, 768)
    shared["bWqkvz"] = _hc(np.concatenate(
        [Wq[:, :, 0:128].reshape(2048, 2048), Wq[:, :, 128:256].reshape(2048, 2048),
         Wq[:, :, 256:512].reshape(2048, 4096), Wq[:, :, 512:768].reshape(2048, 4096)],
        axis=1))
    Wba = inp["L0_attn_b_W_ba"].reshape(2048, 16, 4)
    shared["bWba"] = _hc(np.concatenate(
        [Wba[:, :, 0:2].reshape(2048, 32), Wba[:, :, 2:4].reshape(2048, 32)], axis=1))
    bcw = inp["L0_attn_b_conv_w"]
    shared["bconvw"] = _hc(bcw.reshape(64, 128, 4).transpose(1, 0, 2).reshape(128, 256))
    shared["bWout"] = _hc(inp["L0_attn_b_W_out"])
    small = np.zeros((128, 8), np.float32)
    small[:32, 0] = inp["L0_attn_b_A_log"]
    small[:32, 1] = inp["L0_attn_b_dt_bias"]
    small[:, 2] = inp["L0_attn_b_norm_w"]
    shared["bsmall"] = _hc(small)

    shared["fgT"] = _hc(inp["L0_ffn_gate_w"].T)
    shared["fuT"] = _hc(inp["L0_ffn_up_w"].T)
    shared["fdT"] = _hc(inp["L0_ffn_down_w"].T)
    shared["fgs"] = _hc(inp["L0_ffn_gate_s"].reshape(44, 128).T)
    shared["fus"] = _hc(inp["L0_ffn_up_s"].reshape(44, 128).T)
    shared["fds"] = _hc(inp["L0_ffn_down_s"].reshape(16, 128).T)

    j = np.arange(128)
    cm = np.zeros((128, 4 * 128), np.float32)
    cm[:, 0:128] = np.where(j[None, :] >= j[:, None], 0.0, NEG)
    cm[:, 128:256] = np.where(j[None, :] > j[:, None], 0.0, NEG)
    cm[:, 256:384] = np.where(j[:, None] > j[None, :], 0.0, NEG)
    cm[:, 384:512] = np.eye(128, dtype=np.float32)
    shared["cmask"] = cm

    per_core = []
    for core in range(8):
        b, h = core // 2, core % 2
        m = dict(shared)
        m["xT"] = _hc(x[b, h * T:(h + 1) * T, :].T)
        fl = np.zeros((128, 8), np.float32)
        fl[:, 0] = 1.0 if h == 1 else 0.0
        m["flags"] = fl
        per_core.append(m)
    return per_core



class ExitStackLite:
    def __init__(self):
        self.cms = []

    def __enter__(self):
        return self

    def enter(self, cm):
        v = cm.__enter__()
        self.cms.append(cm)
        return v

    def __exit__(self, *a):
        for cm in reversed(self.cms):
            cm.__exit__(*a)
        return False


def build_program():
    nc = bass.Bass("TRN2", target_bir_lowering=False)
    dram = {}

    def din(name, shape):
        dram[name] = nc.dram_tensor(name, list(shape), F32, kind="ExternalInput").ap()

    din("xT", (D, T))
    din("aWqkv", (D, 8192)); din("aWab", (D, 64)); din("aWo", (4096, D))
    din("aconvw", (128, 256)); din("aconvb", (128, 64))
    din("bWqkvz", (D, 12288)); din("bWba", (D, 64)); din("bconvw", (128, 256))
    din("bWout", (4096, D)); din("bsmall", (128, 8))
    din("fgT", (D, INTER)); din("fuT", (D, INTER)); din("fdT", (INTER, D))
    din("fgs", (128, 44)); din("fus", (128, 44)); din("fds", (128, 16))
    din("cmask", (128, 4 * 128)); din("flags", (128, 8))

    out = nc.dram_tensor("out", [D, T], F32, kind="ExternalOutput").ap()

    ds = {}
    ds["qkvpre"] = nc.dram_tensor("s_qkvpre", [8192, T + 3], BF16, kind="Internal").ap()
    ds["zraw"] = nc.dram_tensor("s_zraw", [NCH, 128, 4096], BF16, kind="Internal").ap()
    ds["ozero"] = nc.dram_tensor("s_ozero", [NCH, 128, 4096], BF16, kind="Internal").ap()
    ds["util"] = nc.dram_tensor("s_util", [NCH, 128, 4096], BF16, kind="Internal").ap()
    ds["hdn"] = nc.dram_tensor("s_hdn", [NIT, 128, T], BF16, kind="Internal").ap()
    ds["qkvpost"] = nc.dram_tensor("s_qkvpost", [8192, T], BF16, kind="Internal").ap()
    ds["dbg"] = nc.dram_tensor("s_dbg", [16, 128, 512], BF16, kind="Internal").ap()
    ds["ssq"] = nc.dram_tensor("s_ssq", [2, 32, T], F32, kind="Internal").ap()
    ds["gcrow"] = nc.dram_tensor("s_gcrow", [8, 32], F32, kind="Internal").ap()
    ds["krows"] = nc.dram_tensor("s_krows", [96, T], F32, kind="Internal").ap()
    ds["ofin"] = nc.dram_tensor("s_ofin", [32, 128, T], BF16, kind="Internal").ap()
    ds["agh_in"] = nc.dram_tensor("s_agh_in", [128, 192], F32, kind="Internal").ap()
    ds["agh_out"] = nc.dram_tensor("s_agh_out", [256, 192], F32, kind="Internal").ap()
    ds["ags_in"] = nc.dram_tensor("s_ags_in", [128, 4096], BF16, kind="Internal").ap()
    ds["ags_out"] = nc.dram_tensor("s_ags_out", [256, 4096], BF16, kind="Internal").ap()

    with ExitStackLite() as stack:
        sb = {}

        def sbuf(name, shape, dt):
            sb[name] = stack.enter(nc.sbuf_tensor("sb_" + name, list(shape), dt))

        def psum(name, shape):
            sb[name] = stack.enter(nc.psum_tensor("ps_" + name, list(shape), F32))

        sbuf("xT", (128, NFT * T), F32)
        sbuf("regA", (128, 16384), BF16)
        sbuf("regB", (128, 16384), BF16)
        sbuf("regC", (128, 4096), BF16)
        sbuf("regD", (128, 3072), BF16)
        sbuf("regE", (128, 2048), BF16)
        sbuf("regF", (128, 1024), F32)
        sbuf("rows", (128, 2048), F32)
        sbuf("scales", (128, 128), F32)
        sbuf("cmask", (128, 4 * 128), F32)
        sbuf("consts", (128, 16), F32)
        sbuf("onesr", (1, 128), F32)
        sbuf("onesb", (128, 8), BF16)
        sbuf("idb", (128, 128), BF16)
        sbuf("flags", (128, 8), F32)
        sbuf("kTf", (128, NCH * 256), F32)
        sbuf("k2d", (128, 1024), F32)
        sbuf("k2d2", (128, 1024), F32)
        sbuf("gates", (64, 1024), F32)
        sbuf("expgc", (128, NCH * 32), F32)
        sbuf("halocol", (128, 192), F32)
        sbuf("hrecv", (128, 192), F32)
        sbuf("halob", (128, 192), BF16)
        sbuf("convw", (128, 256), F32)
        sbuf("convb", (128, 64), F32)
        sbuf("ATd", (128, 256), BF16)
        sbuf("NTn", (128, 256), BF16)
        sbuf("Nn", (128, 256), BF16)
        sbuf("sq", (128, 512), BF16)
        sbuf("sq2", (128, 512), BF16)
        sbuf("xb", (128, 768), BF16)
        sbuf("xb2", (128, 768), BF16)
        sbuf("ktok", (128, 256), BF16)
        sbuf("u0t", (128, 256), BF16)
        sbuf("qpf", (128, 256), BF16)
        sbuf("pp", (128, 512), BF16)
        sbuf("ifacbc", (128, 256), BF16)
        sbuf("regG", (128, 1024), F32)
        sbuf("otn", (128, 256), BF16)
        sbuf("zsil", (128, 128), BF16)
        sbuf("normc", (128, 64), F32)
        sbuf("sqscr", (128, 512), BF16)
        sbuf("bsmall", (128, 8), F32)
        psum("p0", (128, 512)); psum("p1", (128, 512))
        psum("p2", (128, 512)); psum("p3", (128, 512))
        psum("p4", (128, 512)); psum("p5", (128, 512))
        psum("p6", (128, 512))
        sb["p7"] = stack.enter(nc.psum_tensor("ps_p7", [128, 1024], BF16))

        sems = {}
        for name in ("tensor", "vector", "scalar", "gpsimd", "sync", "cc"):
            sems[name] = stack.enter(nc.semaphore(name))
        sems["dma_pool"] = [stack.enter(nc.semaphore("dmach%d" % i))
                            for i in range(64)]

        with nc.Block() as block:
            K = Ker(nc, block, sems)
            emit_all(K, sb, dram, ds, out)
            K.flush()

    return nc



def emit_init(K, sb, dram):
    toks = [
        K.dma("dS", sb["cmask"][:, :], dram["cmask"][:, :], ch="init"),
        K.dma("dS", sb["flags"][:, :], dram["flags"][:, :], ch="init"),
    ]
    for ft in range(NFT):
        toks.append(K.dma("dS", sb["xT"][:, ft * T:(ft + 1) * T],
                          dram["xT"][ft * 128:(ft + 1) * 128, :], ch="xt%d" % (ft % 4)))
    ld = list(toks)
    toks.append(K.v(lambda v: v.memset(sb["consts"][:, 0:1], 1.0)))
    toks.append(K.v(lambda v: v.memset(sb["consts"][:, 1:2], 1e-5)))
    toks.append(K.v(lambda v: v.memset(sb["consts"][:, 2:3], 1e-6)))
    toks.append(K.v(lambda v: v.memset(sb["consts"][:, 3:4], 1e-18)))
    toks.append(K.v(lambda v: v.memset(sb["consts"][:, 4:5], 0.0)))
    toks.append(K.dma("dS", sb["bsmall"][:, :], dram["bsmall"][:, :], ch="init"))
    toks.append(K.v(lambda v: v.memset(sb["onesb"][:, :], 1.0)))
    toks.append(K.v(lambda v: v.memset(sb["onesr"][:, :], 1.0)))
    toks.append(K.v(lambda v: v.tensor_copy(sb["idb"][:, :], sb["cmask"][:, 384:512]),
                    deps=ld))
    return toks


def emit_ln(K, sb, dep_x):
    xT, rows, lnx = sb["xT"], sb["rows"], sb["regA"]
    ones = sb["consts"][:, 0:1]
    eps = sb["consts"][0:1, 1:2]
    outs = []
    bankfree = None
    for half in range(2):
        t0 = half * 512
        p_s, p_q = sb["p6"], sb["p5"]
        mu = rows[0:1, 0:512]
        var = rows[0:1, 512:1024]
        rinv = rows[0:1, 1024:1536]
        cc = rows[0:1, 1536:2048]
        tsum = None
        sq_toks = []
        for ft in range(NFT):
            xs = xT[:, ft * T + t0: ft * T + t0 + 512]
            sq = sb["regF"][:, (ft % 2) * 512:(ft % 2) * 512 + 512]
            prev_mm = sq_toks[-1] if sq_toks else None
            dsq = K.v(lambda v, xs=xs, sq=sq: v.tensor_tensor(sq, xs, xs, ALU.mult),
                      deps=list(dep_x) + ([prev_mm] if prev_mm else []))
            tsum = K.t(lambda t, xs=xs, ft=ft: t.matmul(
                p_s[0:1, :], ones, xs, start=(ft == 0), stop=(ft == NFT - 1)),
                deps=list(dep_x) + ([bankfree] if (ft == 0 and bankfree) else []))
            tq = K.t(lambda t, sq=sq, ft=ft: t.matmul(
                p_q[0:1, :], ones, sq, start=(ft == 0), stop=(ft == NFT - 1)),
                deps=[dsq])
            sq_toks.append(tq)
        tq_last = sq_toks[-1]
        d1 = K.v(lambda v: v.tensor_scalar(mu, p_s[0:1, :], 1.0 / D, None, ALU.mult),
                 deps=[tsum, tq_last])
        d2 = K.v(lambda v: v.tensor_scalar(var, p_q[0:1, :], 1.0 / D, None, ALU.mult),
                 deps=[tsum, tq_last])
        d3 = K.v(lambda v: v.tensor_tensor(rinv, mu, mu, ALU.mult), deps=[d1])
        d4 = K.v(lambda v: v.tensor_tensor(var, var, rinv, ALU.subtract), deps=[d2, d3])
        d5 = K.s(lambda s: s.activation(out=var, in_=var, func=AF.Ln, scale=1.0,
                                        bias=eps), deps=[d4])
        d5b = K.v(lambda v: v.tensor_scalar(var, var, -0.5, None, ALU.mult),
                  deps=[d5])
        d6 = K.s(lambda s: s.activation(out=rinv, in_=var, func=AF.Exp),
                 deps=[d5b])
        d7 = K.v(lambda v: v.scalar_tensor_tensor(cc, mu, -1.0, rinv, ALU.mult,
                                                  ALU.mult), deps=[d6])
        onesr = sb["onesr"][:, :]
        br = K.t(lambda t: t.matmul(p_s[:, :], onesr, rinv, start=True, stop=True),
                 deps=[d6])
        bc = K.t(lambda t: t.matmul(p_q[:, :], onesr, cc, start=True, stop=True),
                 deps=[d7])
        for ft in range(NFT):
            xs = xT[:, ft * T + t0: ft * T + t0 + 512]
            dst = lnx[:, ft * T + t0: ft * T + t0 + 512]
            tmp = sb["regF"][:, (ft % 2) * 512:(ft % 2) * 512 + 512]
            a1 = K.v(lambda v, xs=xs, tmp=tmp: v.tensor_tensor(tmp, xs, p_s[:, :],
                                                               ALU.mult), deps=[br])
            a2 = K.v(lambda v, tmp=tmp, dst=dst: v.tensor_tensor(dst, tmp, p_q[:, :],
                                                                 ALU.add), deps=[bc, a1])
            outs.append(a2)
        bankfree = outs[-1]
    return outs


def emit_ffn(K, sb, dram, ds, dep_in):
    ln_done = emit_ln(K, sb, dep_in)
    if FFN_STAGE == 1:
        return ln_done
    sc = [K.dma("dS", sb["scales"][:, 0:44], dram["fgs"][:, :], ch="init"),
          K.dma("dS", sb["scales"][:, 44:88], dram["fus"][:, :], ch="init"),
          K.dma("dS", sb["scales"][:, 88:104], dram["fds"][:, :], ch="init")]
    lnx = sb["regA"]
    slab_tok = {}
    prev_use = {}
    hdn_toks = []
    ev_prev = {}
    wg = sb["regB"][:, 0:8192]
    wu = sb["regB"][:, 8192:16384]
    wd = sb["regE"]
    hdnb = sb["regD"][:, 0:2048]
    hdnc2 = [sb["regC"][:, 0:1024], sb["regC"][:, 1024:2048]]
    for m in range(NIT):
        mg, mloc = m // 2, m % 2
        slot = mg % 2
        if mloc == 0:
            depw = [prev_use.get(("g", slot)), prev_use.get(("u", slot))]
            src = dram["fgT"][:, mg * 256:(mg + 1) * 256].rearrange(
                "(k p) c -> p k c", p=128)
            dst = wg[:, slot * 4096:(slot + 1) * 4096].rearrange(
                "p (k c) -> p k c", k=16)
            slab_tok[("g", mg)] = K.dma("dW", dst, src, deps=[d for d in depw if d],
                                        ch="wg%d" % slot)
            src = dram["fuT"][:, mg * 256:(mg + 1) * 256].rearrange(
                "(k p) c -> p k c", p=128)
            dst = wu[:, slot * 4096:(slot + 1) * 4096].rearrange(
                "p (k c) -> p k c", k=16)
            slab_tok[("u", mg)] = K.dma("dW", dst, src, deps=[d for d in depw if d],
                                        ch="wu%d" % slot)
        tg = tu = None
        for half in range(2):
            pg = sb["p0"] if half == 0 else sb["p1"]
            pu = sb["p2"] if half == 0 else sb["p3"]
            for k in range(16):
                lg = wg[:, slot * 4096 + k * 256 + mloc * 128:
                        slot * 4096 + k * 256 + mloc * 128 + 128]
                lu = wu[:, slot * 4096 + k * 256 + mloc * 128:
                        slot * 4096 + k * 256 + mloc * 128 + 128]
                rhs = lnx[:, k * T + half * 512: k * T + half * 512 + 512]
                extra = []
                if k == 0:
                    extra = ln_done + [slab_tok[("g", mg)], ev_prev.get(("g", half))]
                tg = K.t(lambda t, lg=lg, rhs=rhs, pg=pg, k=k: t.matmul(
                    pg[:, :], lg, rhs, start=(k == 0), stop=(k == 15)),
                    deps=[slab_tok[("g", mg)]] + [e for e in extra if e])
                extra = []
                if k == 0:
                    extra = ln_done + [slab_tok[("u", mg)], ev_prev.get(("u", half))]
                tu = K.t(lambda t, lu=lu, rhs=rhs, pu=pu, k=k: t.matmul(
                    pu[:, :], lu, rhs, start=(k == 0), stop=(k == 15)),
                    deps=[slab_tok[("u", mg)]] + [e for e in extra if e])
            gs = sb["scales"][:, m:m + 1]
            us = sb["scales"][:, 44 + m:44 + m + 1]
            sg = sb["regF"][:, half * 512: half * 512 + 512]
            e1 = K.s(lambda s, pg=pg, sg=sg, gs=gs: s.activation(
                out=sg, in_=pg[:, :], func=AF.Silu, scale=gs),
                deps=[tg, tu] + sc + [ev_prev.get(("u", half))])
            dst = hdnc2[m % 2][:, half * 512: half * 512 + 512]
            e2 = K.v(lambda v, pu=pu, us=us, sg=sg, dst=dst: v.scalar_tensor_tensor(
                dst, pu[:, :], us, sg, ALU.mult, ALU.mult),
                deps=[e1, hdn_toks[m - 2] if m >= 2 else None])
            ev_prev[("g", half)] = e1
            ev_prev[("u", half)] = e2
        prev_use[("g", slot)] = tg
        prev_use[("u", slot)] = tu
        hdn_toks.append(K.dma("dS", ds["hdn"][m, :, :], hdnc2[m % 2][:, :],
                              deps=[ev_prev[("g", 0)], ev_prev[("u", 0)],
                                    ev_prev[("g", 1)], ev_prev[("u", 1)]],
                              ch="hdnst%d" % (m % 2)))
    if FFN_STAGE == 2:
        return hdn_toks
    res = []
    hprev = {}
    dprev = {}
    bank_free = {}
    psl = [sb["p%d" % i] for i in range(4)]
    guard0 = [ev_prev[("g", 0)], ev_prev[("u", 0)],
              ev_prev[("g", 1)], ev_prev[("u", 1)]]
    for grp in range(4):
        for half in range(2):
            for i in range(NIT):
                hslot = i % 4
                hb = hdnb[:, hslot * 512: hslot * 512 + 512]
                dep_h = [hprev.get(hslot), hdn_toks[i]]
                ht = K.dma("dS", hb,
                           ds["hdn"][i, :, half * 512:(half + 1) * 512],
                           deps=[d for d in dep_h if d],
                           ch="hdnb%d" % hslot)
                wslot = i % 4
                src = dram["fdT"][i * 128:(i + 1) * 128, grp * 512:(grp + 1) * 512]
                wt = K.dma("dW", wd[:, wslot * 512:(wslot + 1) * 512], src,
                           deps=[d for d in [dprev.get(wslot)] if d],
                           ch="wd%d" % wslot)
                last_mm = None
                for dmi in range(4):
                    lw = wd[:, wslot * 512 + dmi * 128:
                            wslot * 512 + dmi * 128 + 128]
                    extra = []
                    if i == 0:
                        extra = [bank_free.get(dmi)] if bank_free else guard0
                    last_mm = K.t(lambda t, lw=lw, hb=hb, p=psl[dmi], i=i: t.matmul(
                        p[:, :], lw, hb, start=(i == 0), stop=(i == NIT - 1)),
                        deps=[ht, wt] + [e for e in extra if e])
                hprev[hslot] = last_mm
                dprev[wslot] = last_mm
            for dmi in range(4):
                dm = grp * 4 + dmi
                dsc = sb["scales"][:, 88 + dm:88 + dm + 1]
                xs = sb["xT"][:, dm * T + half * 512: dm * T + half * 512 + 512]
                e = K.v(lambda v, p=psl[dmi], dsc=dsc, xs=xs: v.scalar_tensor_tensor(
                    xs, p[:, :], dsc, xs, ALU.mult, ALU.add), deps=[last_mm])
                bank_free[dmi] = e
                res.append(e)
    return res




GEO_A = dict(name="a", H=16, KH=16, R=1, DK=128, DV=256, NM=64,
             WQKV="aWqkv", WAB="aWab", WO="aWo", CW="aconvw", CB="aconvb")
GEO_B = dict(name="b", H=32, KH=16, R=2, DK=128, DV=128, NM=96,
             WQKV="bWqkvz", WAB="bWba", WO="bWout", CW="bconvw", CB=None)

CC_GROUPS = [[0, 1], [2, 3], [4, 5], [6, 7]]
MIXER_STAGE = 9


def emit_mixer(K, sb, dram, ds, dep_in, g):
    ln_done = emit_ln(K, sb, dep_in)
    lnx = sb["regA"]

    gw = sb["regB"][:, 6144:7168]
    src = dram[g["WAB"]][:, :].rearrange("(k p) c -> p k c", p=128)
    gwt = K.dma("dW", gw.rearrange("p (k c) -> p k c", k=16), src, ch="gw")
    gtoks = []
    for half in range(2):
        tg = None
        for k in range(16):
            rhs = lnx[:, k * T + half * 512: k * T + half * 512 + 512]
            tg = K.t(lambda t, k=k, rhs=rhs: t.matmul(
                sb["p6"][0:64, :], gw[:, k * 64:(k + 1) * 64], rhs,
                start=(k == 0), stop=(k == 15)),
                deps=[gwt] + (ln_done if k == 0 else []) + (gtoks if k == 0 else []))
        gtoks.append(K.v(lambda v, half=half: v.tensor_copy(
            sb["gates"][:, half * 512: half * 512 + 512], sb["p6"][0:64, :]),
            deps=[tg]))

    slabs = [sb["regB"][:, 0:2048], sb["regB"][:, 2048:4096],
             sb["regB"][:, 7168:9216], sb["regB"][:, 9216:11264]]
    stg = sb["regB"][:, 4096:6144]
    prev_mm = [None, None, None, None]
    stg_free = [None] * 4
    bankq = {i: [] for i in range(4)}
    halo_toks = []
    pj_done = []
    for m in range(g["NM"]):
        slot = m % 4
        wsl = slabs[slot]
        src = dram[g["WQKV"]][:, m * 128:(m + 1) * 128].rearrange(
            "(k p) c -> p k c", p=128)
        wt = K.dma("dW", wsl.rearrange("p (k c) -> p k c", k=16), src,
                   deps=[prev_mm[slot]], ch="wq%d" % slot)
        for half in range(2):
            pq = sb["p%d" % (2 * (m % 2) + half)]
            tg = None
            for k in range(16):
                rhs = lnx[:, k * T + half * 512: k * T + half * 512 + 512]
                tg = K.t(lambda t, k=k, rhs=rhs, pq=pq, wsl=wsl: t.matmul(
                    pq[:, :], wsl[:, k * 128:(k + 1) * 128], rhs,
                    start=(k == 0), stop=(k == 15)),
                    deps=[wt] + (ln_done if k == 0 else []) +
                         (bankq[2 * (m % 2) + half] if k == 0 else []) +
                         ([stg_free[2 * (m % 2) + half]] if k == 0 else []))
            prev_mm[slot] = tg
            ssl = 2 * (m % 2) + half
            sstg = stg[:, ssl * 512:(ssl + 1) * 512]
            e = K.s(lambda s, pq=pq, sstg=sstg: s.activation(
                out=sstg, in_=pq[:, :], func=AF.Copy),
                deps=[tg, stg_free[ssl]])
            bankq[ssl] = [e]
            if m < 64:
                dst = ds["qkvpre"][m * 128:(m + 1) * 128,
                                   3 + half * 512: 3 + half * 512 + 512]
                dt_ = K.dma("dS", dst, sstg, deps=[e], ch="qst%d" % ssl)
            else:
                zt = m - 64
                dst = ds["zraw"].rearrange("c p (zt j) -> p zt c j", j=128)[
                    :, zt, half * 4:(half + 1) * 4, :]
                dt_ = K.dma("dS", dst, sstg.rearrange("p (c j) -> p c j", j=128),
                            deps=[e], ch="qst%d" % ssl)
            stg_free[ssl] = dt_
            pj_done.append(dt_)
            if m < 64 and half == 1:
                halo_toks.append(K.v(lambda v, pq=pq, m=m: v.tensor_copy(
                    sb["halocol"][:, m * 3:(m + 1) * 3], pq[:, 509:512]),
                    deps=[tg]))
                bankq[ssl] = bankq[ssl] + [halo_toks[-1]]
    d1 = K.dma("dS", ds["agh_in"][:, :], sb["halocol"][:, :], deps=halo_toks,
               ch="ag")
    cc1 = K.collective("AllGather", ALU.bypass, [ds["agh_in"][:, :]],
                       [ds["agh_out"][:, :]], CC_GROUPS, None, deps=[d1])
    d2 = K.dma("dS", sb["hrecv"][:, :], ds["agh_out"][0:128, :], deps=[cc1],
               ch="ag")
    hb = K.v(lambda v: v.tensor_scalar(sb["halob"][:, :], sb["hrecv"][:, :],
                                       sb["flags"][:, 0:1], None, ALU.mult),
             deps=[d2])

    cwt = K.dma("dS", sb["convw"][:, :], dram[g["CW"]][:, :], ch="cw")
    cbt = None
    if g["CB"]:
        cbt = K.dma("dS", sb["convb"][:, :], dram[g["CB"]][:, :], ch="cw")
    pre = sb["regB"][:, 0:8216]
    cdiag = sb["regE"]
    cstg = sb["regB"][:, 12288:16384]
    pre_free = [None, None]
    cd_free = [None, None]
    cstg_free = [None, None]
    pc_free = [None, None]
    ssq_evac = None
    ssq_dma = []
    qk_done = []
    for grp in range(16):
        gslot = grp % 2
        psrc = ds["qkvpre"][grp * 512:(grp + 1) * 512, 3:].rearrange(
            "(t p) c -> p t c", p=128)
        pdep = [pre_free[gslot], hb] + (list(stg_free) if grp == 0 else [])
        pt = K.dma("dS", pre[:, gslot * 4108:(gslot + 1) * 4108].rearrange(
            "p (t c) -> p t c", t=4)[:, :, 3:], psrc,
            deps=[d for d in pdep if d], ch="pre%d" % gslot)
        hm = K.v(lambda v, gslot=gslot, grp=grp: v.tensor_copy(
            pre[:, gslot * 4108:(gslot + 1) * 4108].rearrange(
                "p (t c) -> p t c", t=4)[:, :, 0:3],
            sb["halob"][:, grp * 12:(grp + 1) * 12].rearrange(
                "p (t c) -> p t c", t=4)),
            deps=[pt, hb])
        for tl in range(4):
            t_ = grp * 4 + tl
            dslot = t_ % 2
            cslot = t_ % 2
            dtk = []
            for tap in range(4):
                dtk.append(K.v(lambda v, dslot=dslot, tap=tap, t_=t_: v.tensor_scalar(
                    cdiag[:, dslot * 512 + tap * 128: dslot * 512 + (tap + 1) * 128],
                    sb["cmask"][:, 384:512],
                    sb["convw"][:, t_ * 4 + tap:t_ * 4 + tap + 1],
                    None, ALU.mult), deps=[cwt, cd_free[dslot]]))
            last_sil = None
            sils = []
            for pan in range(4):
                pslot = (t_ * 4 + pan) % 2
                pc = sb["p%d" % (4 + pslot)]
                tm = None
                for tap in range(4):
                    rhs = pre[:, gslot * 4108 + tl * 1027 + pan * 256 + tap:
                              gslot * 4108 + tl * 1027 + pan * 256 + tap + 256]
                    tm = K.t(lambda t, pc=pc, dslot=dslot, tap=tap, rhs=rhs: t.matmul(
                        pc[:, 0:256], cdiag[:, dslot * 512 + tap * 128:
                                            dslot * 512 + (tap + 1) * 128], rhs,
                        start=(tap == 0), stop=(tap == 3)),
                        deps=dtk + [hm, pc_free[pslot]])
                cd_free[dslot] = tm
                dst = cstg[:, cslot * 2048 + pan * 256: cslot * 2048 + (pan + 1) * 256]
                if g["CB"]:
                    e = K.s(lambda s, pc=pc, dst=dst, t_=t_: s.activation(
                        out=dst, in_=pc[:, 0:256], func=AF.Silu,
                        bias=sb["convb"][:, t_:t_ + 1]),
                        deps=[tm, cbt, cstg_free[cslot]])
                else:
                    e = K.s(lambda s, pc=pc, dst=dst: s.activation(
                        out=dst, in_=pc[:, 0:256], func=AF.Silu),
                        deps=[tm, cstg_free[cslot]])
                pc_free[pslot] = e
                last_sil = e
                sils.append(e)
            if t_ < 32:
                evs = []
                for pair in range(2):
                    sqs = []
                    for j in range(2):
                        pan = pair * 2 + j
                        dst = cstg[:, cslot * 2048 + pan * 256:
                                   cslot * 2048 + (pan + 1) * 256]
                        sqs.append(K.v(lambda v, dst=dst, j=j: v.tensor_mul(
                            sb["sqscr"][:, j * 256:(j + 1) * 256], dst, dst),
                            deps=[sils[pan], ssq_evac]))
                    tq = K.t(lambda t: t.matmul(
                        sb["p6"][0:1, 0:512], sb["onesb"][:, 0:1],
                        sb["sqscr"][:, 0:512], start=True, stop=True),
                        deps=sqs + evs)
                    ssq_evac = tq
                    evs.append(K.v(lambda v, pair=pair: v.tensor_copy(
                        sb["rows"][0:1, pair * 512:(pair + 1) * 512],
                        sb["p6"][0:1, :]), deps=[tq]))
                qk = 0 if t_ < 16 else 1
                qkix = t_ % 16
                if g["R"] == 1:
                    ssq_dma.append(K.dma(
                        "dS", ds["ssq"][qk, qkix, :],
                        sb["rows"][0:1, 0:1024], deps=evs, ch="ssq"))
                else:
                    ssq_dma.append(K.dma(
                        "dS", ds["ssq"][qk, 2 * qkix, :][None, :],
                        sb["rows"][0:1, 0:1024], deps=evs, ch="ssq"))
                    ssq_dma.append(K.dma(
                        "dS", ds["ssq"][qk, 2 * qkix + 1, :][None, :],
                        sb["rows"][0:1, 0:1024], deps=evs, ch="ssq"))
                ssq_evac = ssq_dma[-1]
            qk_done.append(K.dma("dS", ds["qkvpost"][t_ * 128:(t_ + 1) * 128, :],
                                 cstg[:, cslot * 2048: cslot * 2048 + 1024],
                                 deps=[last_sil], ch="cst%d" % cslot))
            cstg_free[cslot] = qk_done[-1]
            if tl == 3:
                pre_free[gslot] = tm

    if MIXER_STAGE <= 1:
        return qk_done + pj_done + gtoks

    kd = emit_kinds(K, sb, dram, ds, g, gtoks, ssq_dma)
    if MIXER_STAGE <= 2:
        return qk_done + [kd["ktf"][-1], kd["egc"][-1]]

    p1 = emit_pass1(K, sb, dram, ds, g, qk_done, kd)
    if MIXER_STAGE <= 3:
        return p1["all"]

    szb = sb["regB"][:, 0:4096]
    d3 = K.dma("dS", ds["ags_in"][:, :], szb, deps=p1["chain"], ch="ag")
    cc2 = K.collective("AllGather", ALU.bypass, [ds["ags_in"][:, :]],
                       [ds["ags_out"][:, :]], CC_GROUPS, None, deps=[d3])
    d4 = K.dma("dS", szb, ds["ags_out"][0:128, :], deps=[cc2] + p1["all"], ch="ag")
    sin_tok = K.v(lambda v: v.tensor_scalar(sb["regC"][:, :], szb,
                                            sb["flags"][:, 0:1], None, ALU.mult),
                  deps=[d4])

    return emit_pass2(K, sb, dram, ds, g, sin_tok, p1)




def emit_softplus(K, sb, out_ap, in_ap, scale, bias_ap, scr, deps):
    hs = out_ap.shape[0]
    ones_col = sb["consts"][96:96 + hs, 0:1]
    scrB = sb["rows"][96:96 + hs, 0:1024]
    if bias_ap is None:
        a0 = K.v(lambda v: v.tensor_scalar(scr, in_ap, scale, None, ALU.mult),
                 deps=deps)
    else:
        a0 = K.v(lambda v: v.tensor_scalar(scr, in_ap, bias_ap, None, ALU.add),
                 deps=deps)
        if scale != 1.0:
            a0 = K.v(lambda v: v.tensor_scalar(scr, scr, scale, None, ALU.mult),
                     deps=[a0])
    a1 = K.v(lambda v: v.scalar_tensor_tensor(scrB, scr, -1.0, scr, ALU.mult,
                                              ALU.min), deps=[a0])
    a2 = K.s(lambda s: s.activation(out=scrB, in_=scrB, func=AF.Exp), deps=[a1])
    a3 = K.s(lambda s: s.activation(out=scrB, in_=scrB, func=AF.Ln, scale=1.0,
                                    bias=ones_col), deps=[a2])
    a4 = K.v(lambda v: v.tensor_scalar_max(scr, scr, 0.0), deps=[a1])
    return K.v(lambda v: v.tensor_add(out_ap, scr, scrB), deps=[a3, a4])


def emit_kinds(K, sb, dram, ds, g, gtoks, ssq_dma):
    k2d, k2d2 = sb["k2d"], sb["k2d2"]
    gates = sb["gates"]
    hs = g["H"]
    A_ = k2d[0:hs, :]
    B_ = k2d[32:32 + hs, :]
    C_ = k2d[64:64 + hs, :]
    NG = k2d[96:96 + hs, :]
    F_ = k2d2[0:hs, :]
    E_ = k2d2[32:32 + hs, :]
    D_ = k2d2[64:64 + hs, :]
    SP = k2d2[96:96 + hs, :]
    scr96 = sb["regF"][96:96 + hs, 0:1024]
    nG0 = sb["regF"][0:hs, 0:1024]
    nG32 = sb["regF"][32:32 + hs, 0:1024]
    nG64 = sb["regF"][64:64 + hs, 0:1024]
    mz1 = K.v(lambda v: v.memset(k2d[:, :], 0.0), deps=gtoks)
    mz2 = K.v(lambda v: v.memset(k2d2[:, :], 0.0), deps=gtoks)
    gtoks = gtoks + [mz1, mz2]
    if g["name"] == "a":
        apre, bpre = gates[0:16, :], gates[32:48, :]
        sp = emit_softplus(K, sb, SP, apre, -1.0, None, scr96, gtoks)
        lnq_c, lnk_c = 0.0, 0.0
        ln_scale = 1.0
        eps_col = 3
    else:
        bpre, apre = gates[0:32, :], gates[32:64, :]
        mexpA = K.s(lambda s: s.activation(out=sb["normc"][96:128, 0:1],
                                           in_=sb["bsmall"][0:32, 0:1],
                                           func=AF.Exp), deps=gtoks)
        dtc = K.v(lambda v: v.tensor_copy(sb["normc"][96:128, 1:2],
                                          sb["bsmall"][0:32, 1:2]), deps=gtoks)
        sp0 = emit_softplus(K, sb, SP, apre, 1.0, sb["normc"][96:128, 1:2],
                            scr96, gtoks + [dtc])
        sp = K.v(lambda v: v.tensor_scalar(SP, SP, sb["normc"][96:128, 0:1],
                                           None, ALU.mult), deps=[sp0, mexpA])
        lnq_c, lnk_c = float(-np.log(128.0)), float(-0.5 * np.log(128.0))
        ln_scale = 1.0 / 128.0
    ngt = []
    for c in range(NCH):
        ngt.append(K.v(lambda v, c=c: v.tensor_tensor_scan(
            NG[:, c * 128:(c + 1) * 128], SP[:, c * 128:(c + 1) * 128],
            SP[:, c * 128:(c + 1) * 128], 0.0, ALU.add, ALU.bypass), deps=[sp]))
    ng_c = [K.v(lambda v: v.tensor_copy(nG0, NG), deps=ngt),
            K.v(lambda v: v.tensor_copy(nG32, NG), deps=ngt),
            K.v(lambda v: v.tensor_copy(nG64, NG), deps=ngt)]
    spb = emit_softplus(K, sb, SP, bpre, -1.0, None, scr96, gtoks + ngt + ng_c)
    sg32 = sb["rows"][32:32 + hs, 0:1024]
    bt0 = K.v(lambda v: v.tensor_scalar(sg32, bpre, -1.0, None, ALU.mult),
              deps=gtoks)
    bt1 = K.s(lambda s: s.activation(out=sg32, in_=sg32, func=AF.Exp), deps=[bt0])
    bt2 = K.v(lambda v: v.tensor_scalar(sg32, sg32, 1.0, None, ALU.add), deps=[bt1])
    beta = K.v(lambda v: v.reciprocal(E_, sg32), deps=[bt2])

    def lnr_load(dst, qk, dep_extra):
        if g["name"] == "a":
            t0 = K.dma("dS", dst, ds["ssq"][qk, 0:16, :],
                       deps=ssq_dma + gtoks + dep_extra, ch="lnr")
            t1 = K.v(lambda v: v.tensor_scalar_max(dst, dst, 1e-18), deps=[t0])
        else:
            t1 = K.dma("dS", dst, ds["ssq"][qk, 0:32, :],
                       deps=ssq_dma + gtoks + dep_extra, ch="lnr")
        return t1

    def lnr_ln(dst, base, cnst, dep):
        if g["name"] == "b":
            dep = K.v(lambda v: v.tensor_scalar(dst, dst, 1.0 / 128.0, 1e-6,
                                                ALU.mult, ALU.add), deps=[dep])
        l_ = K.s(lambda s: s.activation(out=dst, in_=dst, func=AF.Ln,
                                        scale=1.0,
                                        bias=sb["consts"][base:base + hs, 4:5]),
                 deps=[dep])
        return K.v(lambda v: v.tensor_scalar(dst, dst, -0.5, cnst, ALU.mult,
                                             ALU.add), deps=[l_])

    lq = lnr_load(A_, 0, [])
    hq = lnr_ln(A_, 0, lnq_c, lq)
    a_ = K.v(lambda v: v.scalar_tensor_tensor(A_, nG0, -1.0, A_, ALU.mult,
                                              ALU.add), deps=[hq] + ng_c)
    lk = lnr_load(C_, 1, [])
    hk = lnr_ln(C_, 64, lnk_c, lk)
    c_ = K.v(lambda v: v.scalar_tensor_tensor(C_, nG64, 1.0, C_, ALU.mult,
                                              ALU.add), deps=[hk] + ng_c)
    lk2 = lnr_load(B_, 1, [])
    hk2 = lnr_ln(B_, 32, lnk_c, lk2)
    b1 = K.v(lambda v: v.scalar_tensor_tensor(B_, nG32, -1.0, B_, ALU.mult,
                                              ALU.add), deps=[hk2] + ng_c)
    sc32 = K.v(lambda v: v.tensor_copy(nG32, SP), deps=[b1, spb])
    b_ = K.v(lambda v: v.tensor_sub(B_, B_, nG32), deps=[sc32])
    f1 = K.s(lambda s: s.activation(out=F_, in_=B_, func=AF.Exp), deps=[b_])
    f_ = K.v(lambda v: v.tensor_scalar(F_, F_, -1.0, None, ALU.mult), deps=[f1])
    dts = []
    for c in range(NCH):
        dts.append(K.v(lambda v, c=c: v.tensor_scalar(
            D_[:, c * 128:(c + 1) * 128], C_[:, c * 128:(c + 1) * 128],
            nG64[:, c * 128 + 127:c * 128 + 128], None, ALU.subtract),
            deps=[c_] + ng_c))
    dex = K.s(lambda s: s.activation(out=D_, in_=D_, func=AF.Exp), deps=dts)
    gc1 = K.v(lambda v: v.tensor_copy(
        sb["normc"][0:hs, 8:16], NG[:, 127:1024:128]), deps=ngt + ssq_dma)
    gc1b = K.v(lambda v: v.tensor_scalar(sb["normc"][0:hs, 8:16],
                                         sb["normc"][0:hs, 8:16], -1.0, None,
                                         ALU.mult), deps=[gc1])
    gc2 = K.s(lambda s: s.activation(out=sb["normc"][0:hs, 8:16],
                                     in_=sb["normc"][0:hs, 8:16], func=AF.Exp),
              deps=[gc1b])
    gc3 = K.t(lambda t: t.transpose(sb["p6"][0:8, 0:hs], sb["normc"][0:hs, 8:16],
                                    sb["cmask"][0:hs, 384:384 + hs]),
              deps=[gc2] + ssq_dma)
    gc4 = K.v(lambda v: v.tensor_copy(sb["normc"][0:8, 16:16 + hs],
                                      sb["p6"][0:8, 0:hs]), deps=[gc3])
    gc5 = K.dma("dS", ds["gcrow"][:, 0:hs], sb["normc"][0:8, 16:16 + hs],
                deps=[gc4], ch="gc")
    gc6 = K.dma("dS", sb["rows"][0:1, 1024:1024 + 8 * hs].rearrange(
                    "p (a b) -> p a b", a=8),
                ds["gcrow"][None, :, 0:hs], deps=[gc5], ch="gc")
    egc = []
    for c in range(NCH):
        tb = K.t(lambda t, c=c: t.matmul(
            sb["p6"][:, 0:hs], sb["onesr"][:, :],
            sb["rows"][0:1, 1024 + c * hs:1024 + (c + 1) * hs],
            start=True, stop=True), deps=[gc6, egc[-1] if egc else gc4])
        egc.append(K.v(lambda v, c=c: v.tensor_copy(
            sb["expgc"][:, c * 32:c * 32 + hs], sb["p6"][:, 0:hs]), deps=[tb]))
    kr = K.dma("dS", ds["krows"][:, :], k2d[0:96, :], deps=[a_, b_, c_] + dts,
               ch="krows")
    ktf = []
    identf = sb["cmask"][:, 384:512]
    kd_done = [a_, b_, f_, dex, c_]
    for c in range(NCH):
        tt1 = K.t(lambda t, c=c: t.transpose(
            sb["p6"][:, 0:128], k2d[:, c * 128:(c + 1) * 128], identf),
            deps=kd_done + ([ktf[-1]] if ktf else []) + ([egc[-1]] if c == 0 else []))
        e1 = K.v(lambda v, c=c: v.tensor_copy(
            sb["kTf"][:, c * 256:c * 256 + 128], sb["p6"][:, 0:128]), deps=[tt1])
        tt2 = K.t(lambda t, c=c: t.transpose(
            sb["p6"][:, 128:256], k2d2[:, c * 128:(c + 1) * 128], identf),
            deps=kd_done + [e1])
        e2 = K.v(lambda v, c=c: v.tensor_copy(
            sb["kTf"][:, c * 256 + 128:c * 256 + 256], sb["p6"][:, 128:256]),
            deps=[tt2])
        ktf.append(e2)
    return dict(ktf=ktf, egc=egc, krows=[kr])


def emit_pass1(K, sb, dram, ds, g, qk_done, kd):
    H, R, DK, DV = g["H"], g["R"], g["DK"], g["DV"]
    KH = g["KH"]
    XW = DV + 128
    regA, regB = sb["regA"], sb["regB"]
    szb = regB[:, 0:4096]
    ozc = regB[:, 4096:8192]
    ucol = regB[:, 8192:12288]
    kTf, expgc = sb["kTf"], sb["expgc"]
    identf = sb["cmask"][:, 384:512]
    idb = sb["idb"]
    pA = [sb["p0"], sb["p3"]]
    pB = [sb["p1"], sb["p4"]]
    pC = [sb["p2"], sb["p5"]]
    p6 = sb["p6"]
    p7 = sb["p7"]

    z1 = K.v(lambda v: v.memset(szb[:, :], 0.0), deps=qk_done + kd["egc"])
    stok = {h: [z1] for h in range(H)}
    gtok = {h: None for h in range(H)}
    bank = {("A", 0): [z1], ("A", 1): [z1], ("B", 0): [z1], ("B", 1): [z1],
            ("C", 0): [z1], ("C", 1): [z1],
            ("T7", 0): [z1], ("T7", 1): [z1]}
    sbt = {}
    pan_tok = None
    all_t = []
    oz_dma = []
    uc_dma = []
    coll_free = [z1]
    rega_readers = []

    def head_gen(c, ci, h, par, dbg):
        hk = h // R
        cA = kTf[:, c * 256 + 0 + h:c * 256 + 0 + h + 1]
        cB = kTf[:, c * 256 + 32 + h:c * 256 + 32 + h + 1]
        cC = kTf[:, c * 256 + 64 + h:c * 256 + 64 + h + 1]
        cF = kTf[:, c * 256 + 128 + h:c * 256 + 128 + h + 1]
        cE = kTf[:, c * 256 + 160 + h:c * 256 + 160 + h + 1]
        cD = kTf[:, c * 256 + 192 + h:c * 256 + 192 + h + 1]
        egc_c = expgc[:, c * 32 + h:c * 32 + h + 1]
        Qf = regA[:, hk * 256 + ci * 128: hk * 256 + ci * 128 + 128]
        Kf = regA[:, (16 + hk) * 256 + ci * 128: (16 + hk) * 256 + ci * 128 + 128]
        if g["name"] == "a":
            vt = 32 + 2 * h
            Vf = [regA[:, (vt + i) * 256 + ci * 128: (vt + i) * 256 + ci * 128 + 128]
                  for i in range(2)]
        else:
            Vf = [regA[:, (32 + h) * 256 + ci * 128: (32 + h) * 256 + ci * 128 + 128]]
        szb_h = szb[:, h * DV:(h + 1) * DV]
        A_, B_, C_ = pA[par], pB[par], pC[par]
        dexp = sb["regF" if par == 0 else "regG"][:, 0:384]
        bcb = sb["regF" if par == 0 else "regG"][:, 384:768]
        ATd = sb["ATd"][:, par * 128:(par + 1) * 128]
        NTn = sb["NTn"][:, par * 128:(par + 1) * 128]
        Nn = sb["Nn"][:, par * 128:(par + 1) * 128]
        ktok = sb["ktok"][:, par * 128:(par + 1) * 128]
        u0t = sb["u0t"][:, par * 128:(par + 1) * 128]
        qpf = sb["qpf"][:, par * 128:(par + 1) * 128]
        pp = sb["pp"][:, par * 256:(par + 1) * 256]
        ifac = sb["ifacbc"][:, par * 128:(par + 1) * 128]
        xcur = sb["xb"][:, par * 384: par * 384 + XW]
        xnxt = sb["xb2"][:, par * 384: par * 384 + XW]
        sqc = sb["sq"][:, par * 256:(par + 1) * 256]
        sqn = sb["sq2"][:, par * 256:(par + 1) * 256]
        gsl = sb["regC"][:, h * 128:(h + 1) * 128]

        rbC, rbA, rbB = bcb[:, 0:128], bcb[:, 128:256], bcb[:, 256:384]
        rdep = kd["krows"] + sbt.get(("rb", par), [])
        dA = K.dma("dW", rbA,
                   ds["krows"][h:h + 1, c * 128:(c + 1) * 128]
                   .broadcast_to([128, 128]), deps=rdep, ch="rbA%d" % par)
        dB = K.dma("dW", rbB,
                   ds["krows"][32 + h:32 + h + 1, c * 128:(c + 1) * 128]
                   .broadcast_to([128, 128]), deps=rdep, ch="rbB%d" % par)
        dC = K.dma("dW", rbC,
                   ds["krows"][64 + h:64 + h + 1, c * 128:(c + 1) * 128]
                   .broadcast_to([128, 128]), deps=rdep, ch="rbC%d" % par)
        t1 = K.t(lambda t: t.matmul(A_[:, 0:128], Kf, Qf, start=True, stop=True),
                 deps=bank[("A", par)] + [pan_tok])
        t2 = K.t(lambda t: t.matmul(A_[:, 128:256], Kf, Kf, start=True, stop=True),
                 deps=[])
        rega_readers.append(t2)
        K.t(lambda t: t.ldweights(idb[:, :]), deps=[])
        yield
        ddep = sbt.get(("dexp", par), [])
        v4 = K.v(lambda v: v.scalar_tensor_tensor(
            dexp[:, 0:128], rbA, cC, sb["cmask"][:, 0:128],
            ALU.add, ALU.add), deps=[dA] + ddep)
        v5 = K.v(lambda v: v.scalar_tensor_tensor(
            dexp[:, 128:256], rbB, cC, sb["cmask"][:, 128:256],
            ALU.add, ALU.add), deps=[dB])
        v6 = K.v(lambda v: v.scalar_tensor_tensor(
            dexp[:, 256:384], rbC, cB, sb["cmask"][:, 256:384],
            ALU.add, ALU.add), deps=[dC])
        yield
        s1 = K.s(lambda s: s.activation(out=dexp[:, :], in_=dexp[:, :],
                                        func=AF.Exp), deps=[v4, v5, v6])
        s1b = K.s(lambda s: s.activation(out=ifac, in_=rbA,
                                         func=AF.Exp),
                  deps=[dA] + sbt.get(("ifac", par), []))
        sbt[("rb", par)] = [v4, v5, v6, s1b]
        yield
        v7 = K.v(lambda v: v.tensor_tensor(ATd, A_[:, 0:128], dexp[:, 0:128],
                                           ALU.mult),
                 deps=[s1, t2] + sbt.get(("ATd", par), []))
        v8 = K.v(lambda v: v.scalar_tensor_tensor(
            NTn, A_[:, 128:256], -1.0, dexp[:, 128:256], ALU.mult, ALU.mult),
            deps=[s1, t2])
        v9 = K.v(lambda v: v.scalar_tensor_tensor(
            Nn, A_[:, 128:256], -1.0, dexp[:, 256:384], ALU.mult, ALU.mult),
            deps=[s1, t2])
        sbt[("dexp", par)] = [v7, v8, v9]
        bank[("A", par)] = [v7, v8, v9]
        v10 = K.g(lambda v: v.tensor_tensor(qpf, Qf, ifac, ALU.mult),
                  deps=[s1b, pan_tok] + sbt.get(("qpf", par), []))
        sbt[("ifac", par)] = [v10]
        rega_readers.append(v10)
        yield
        if par == 0:
            ktp = p6
            t6 = K.t(lambda t: t.matmul(ktp[:, 0:128], Kf, idb[:, :],
                                        start=True, stop=True),
                     deps=bank[("T7", par)])
        else:
            ktp = p7
            t6 = K.t(lambda t: t.transpose(ktp[:, 0:128], Kf, idb[:, :]),
                     deps=bank[("T7", par)])
        rega_readers.append(t6)
        s2 = K.s(lambda s: s.activation(out=xcur[:, DV:DV + 128],
                                        in_=ktp[:, 0:128], func=AF.Copy,
                                        scale=cF),
                 deps=[t6] + sbt.get(("xb", par), []))
        s3 = K.s(lambda s: s.activation(out=ktok, in_=ktp[:, 0:128],
                                        func=AF.Copy, scale=cD),
                 deps=[t6] + sbt.get(("ktok", par), []))
        yield
        vts = []
        prev_ev = None
        for i, vf in enumerate(Vf):
            if par == 0:
                t7 = K.t(lambda t, vf=vf: t.matmul(
                    ktp[:, 128:256], vf, idb[:, :], start=True, stop=True),
                    deps=([s2, s3] if i == 0 else []) +
                         ([prev_ev] if prev_ev else []))
            else:
                t7 = K.t(lambda t, vf=vf: t.transpose(ktp[:, 128:256], vf,
                                                      idb[:, :]),
                         deps=([s2, s3] if i == 0 else []) +
                              ([prev_ev] if prev_ev else []))
            rega_readers.append(t7)
            ev = K.v(lambda v, i=i: v.tensor_copy(
                xcur[:, i * 128:(i + 1) * 128], ktp[:, 128:256]), deps=[t7])
            ec = K.v(lambda v, i=i: v.tensor_scalar(
                xcur[:, i * 128:(i + 1) * 128], xcur[:, i * 128:(i + 1) * 128],
                cE, None, ALU.mult), deps=[ev])
            prev_ev = ec
            vts.append(ec)
        bank[("T7", par)] = vts + [s2, s3]
        yield
        lhsT_T, lhsT_A = NTn, Nn
        acur = [v8, v9] + vts
        xtok = vts + [s2]
        for lev in range(7):
            tx = K.t(lambda t, lhsT_T=lhsT_T, xc=xcur: t.matmul(
                A_[:, 0:XW], lhsT_T[:, :], xc[:, 0:XW], start=True, stop=True),
                deps=acur + xtok + bank[("A", par)])
            ex = K.v(lambda v, xn=xnxt, xc=xcur: v.tensor_tensor(
                xn[:, 0:XW], A_[:, 0:XW], xc[:, 0:XW], ALU.add), deps=[tx])
            bank[("A", par)] = [ex]
            xtok = [ex]
            xcur, xnxt = xnxt, xcur
            if lev % 2 == 1:
                K.t(lambda t: t.ldweights(idb[:, :]), deps=[])
            if lev < 6:
                ta = K.t(lambda t, lhsT_T=lhsT_T, lhsT_A=lhsT_A: t.matmul(
                    B_[:, 0:128], lhsT_T[:, :], lhsT_A[:, :],
                    start=True, stop=True), deps=acur + bank[("B", par)])
                tb = K.t(lambda t, lhsT_T=lhsT_T, lhsT_A=lhsT_A: t.matmul(
                    B_[:, 128:256], lhsT_A[:, :], lhsT_T[:, :],
                    start=True, stop=True), deps=[])
                if lev % 2 == 0:
                    ea = K.v(lambda v, sq=sqn: v.tensor_copy(sq[:, 0:256],
                                                             B_[:, 0:256]),
                             deps=[ta, tb])
                else:
                    ea = K.s(lambda s, sq=sqn: s.activation(
                        out=sq[:, 0:256], in_=B_[:, 0:256], func=AF.Copy),
                        deps=[ta, tb])
                bank[("B", par)] = [ea]
                acur = [ea]
                lhsT_A = sqn[:, 0:128]
                lhsT_T = sqn[:, 128:256]
                sqc, sqn = sqn, sqc
            yield
        Xf = xcur
        xft = xtok[0]
        sbt[("xb", par)] = xtok
        t9 = K.t(lambda t: t.matmul(C_[:, 0:128], Xf[:, DV:DV + 128], ATd,
                                    start=True, stop=False),
                 deps=[xft, v7] + bank[("C", par)])
        t10 = K.t(lambda t: t.matmul(C_[:, 0:128], idb[:, :], qpf,
                                     start=False, stop=True), deps=[v10])
        e_u0 = K.v(lambda v: v.tensor_copy(u0t, C_[:, 0:128]),
                   deps=[t10] + sbt.get(("u0t", par), []))
        yield
        if c == 0:
            t11 = K.t(lambda t: t.matmul(C_[:, 128:256], idb[:, :], u0t,
                                         start=True, stop=True), deps=[e_u0])
        else:
            t11 = K.t(lambda t: t.matmul(C_[:, 128:256], gsl, u0t,
                                         start=True, stop=True),
                      deps=[e_u0, gtok[h]])
        e_uc = K.s(lambda s: s.activation(out=ucol[:, h * 128:(h + 1) * 128],
                                          in_=C_[:, 128:256], func=AF.Copy),
                   deps=[t11] + coll_free)
        yield
        t12 = K.t(lambda t: t.matmul(C_[:, 256:384], ktok, Xf[:, DV:DV + 128],
                                     start=True, stop=True),
                  deps=[s3, xft, e_uc])
        t13 = K.t(lambda t: t.matmul(C_[:, 384:512], Xf[:, DV:DV + 128], ktok,
                                     start=True, stop=True), deps=[])
        e_p = K.v(lambda v: v.scalar_tensor_tensor(
            pp[:, 0:128], identf, egc_c, C_[:, 256:384], ALU.mult, ALU.add),
            deps=[t12, t13] + sbt.get(("pp", par), []))
        e_pt = K.v(lambda v: v.scalar_tensor_tensor(
            pp[:, 128:256], identf, egc_c, C_[:, 384:512], ALU.mult, ALU.add),
            deps=[t13])
        sbt[("u0t", par)] = [t11, t12, t13]
        yield
        if c < NCH - 1:
            if c == 0:
                t14 = K.t(lambda t: t.matmul(B_[:, 256:384], pp[:, 128:256],
                                             idb[:, :], start=True, stop=True),
                          deps=[e_pt, bank[("B", par)][0]])
            else:
                t14 = K.t(lambda t: t.matmul(B_[:, 256:384], pp[:, 128:256],
                                             gsl, start=True, stop=True),
                          deps=[e_pt, gtok[h], bank[("B", par)][0]])
            ng = K.v(lambda v: v.tensor_copy(gsl, B_[:, 256:384]),
                     deps=[t14, t11])
            gtok[h] = ng
            bank[("B", par)] = bank[("B", par)] + [ng]
        yield
        cho = 256 if DV == 256 else 128
        to1 = K.t(lambda t: t.matmul(C_[:, 0:DV], ATd, Xf[:, 0:DV],
                                     start=True, stop=False),
                  deps=[xft, v7, e_u0, e_uc, e_p, e_pt])
        to2 = K.t(lambda t: t.matmul(C_[:, 0:DV], u0t, szb_h,
                                     start=False, stop=True),
                  deps=[e_u0] + stok[h])
        e_oz = K.s(lambda s: s.activation(out=ozc[:, h * DV:(h + 1) * DV],
                                          in_=C_[:, 0:DV], func=AF.Copy),
                   deps=[to2] + coll_free)
        K.t(lambda t: t.ldweights(idb[:, :]), deps=[])
        yield
        tc1 = K.t(lambda t: t.matmul(C_[:, cho:cho + DV], ktok, Xf[:, 0:DV],
                                     start=True, stop=False),
                  deps=[s3, xft, e_oz])
        tc2 = K.t(lambda t: t.matmul(C_[:, cho:cho + DV], pp[:, 128:256], szb_h,
                                     start=False, stop=True),
                  deps=[e_pt] + stok[h])
        e_s2 = K.v(lambda v: v.tensor_copy(szb_h, C_[:, cho:cho + DV]),
                   deps=[tc2])
        stok[h] = [e_s2]
        bank[("C", par)] = [e_oz, e_s2]
        sbt[("ktok", par)] = [tc1]
        sbt[("qpf", par)] = [t10]
        sbt[("ATd", par)] = [to1]
        all_t.append(e_s2)
        return

    for pan in range(4):
        src_ = ds["qkvpost"][:, pan * 256:(pan + 1) * 256].rearrange(
            "(t p) c -> p t c", p=128)
        pan_dep = qk_done + ([pan_tok] if pan_tok else []) + kd["ktf"] + \
            rega_readers
        rega_readers = []
        pan_tok = K.dma("dS", regA.rearrange("p (t c) -> p t c", t=64), src_,
                        deps=[d for d in pan_dep if d], ch="pcv")
        for ci in range(2):
            c = pan * 2 + ci
            gens = []
            order = list(range(H))
            head_done = []
            gi = 0
            active = []
            while active or gi < len(order):
                while len(active) < 2 and gi < len(order):
                    h = order[gi]
                    active.append(head_gen(c, ci, h, gi % 2, None))
                    gi += 1
                done_idx = []
                for k_, gen_ in enumerate(active):
                    try:
                        next(gen_)
                    except StopIteration:
                        done_idx.append(k_)
                for k_ in reversed(done_idx):
                    active.pop(k_)
            head_done = [stok[h][0] for h in range(H)] + \
                [Tok(K.st["scalar"]["sem"], K.st["scalar"]["cnt"]),
                 Tok(K.st["vector"]["sem"], K.st["vector"]["cnt"])]
            oz_dma.append(K.dma("dS", ds["ozero"][c, :, 0:H * DV],
                                ozc[:, 0:H * DV], deps=head_done, ch="ozst"))
            uc_dma.append(K.dma("dS", ds["util"][c, :, 0:H * 128],
                                ucol[:, 0:H * 128], deps=head_done, ch="ucst"))
            coll_free = [oz_dma[-1], uc_dma[-1]]
    chain = [stok[h][0] for h in range(H)]
    return dict(chain=chain + oz_dma + uc_dma, all=all_t + oz_dma + uc_dma,
                oz=oz_dma, uc=uc_dma)


def emit_pass2(K, sb, dram, ds, g, sin_tok, p1):
    H, R, DK, DV = g["H"], g["R"], g["DK"], g["DV"]
    regA, regB = sb["regA"], sb["regB"]
    zc = regB[:, 0:4096]
    ozc = regB[:, 4096:8192]
    ucol = regB[:, 8192:12288]
    otrb = regB[:, 12288:16384]
    sinb = sb["regC"]
    gfc = regA[:, 0:4096]
    normc = sb["normc"]
    idb = sb["idb"]
    p0, p1b = sb["p0"], sb["p7"]
    ld_free = [sin_tok]
    gf_dma = []
    for c in range(NCH):
        l1 = K.dma("dS", ozc[:, 0:H * DV], ds["ozero"][c, :, 0:H * DV],
                   deps=p1["oz"] + ld_free, ch="oz2")
        l2 = K.dma("dS", ucol[:, 0:H * 128], ds["util"][c, :, 0:H * 128],
                   deps=p1["uc"] + ld_free, ch="uc2")
        lz = None
        if g["name"] == "b":
            lz = K.dma("dS", zc[:, :], ds["zraw"][c, :, :],
                       deps=ld_free, ch="z2")
        w1 = []
        bank0 = [sin_tok]
        for h in range(H):
            tcr = K.t(lambda t, h=h: t.matmul(
                p0[:, 0:DV], ucol[:, h * 128:(h + 1) * 128],
                sinb[:, h * DV:(h + 1) * DV], start=True, stop=True),
                deps=[l2, sin_tok] + bank0)
            eo = K.v(lambda v, h=h: v.tensor_tensor(
                otrb[:, h * DV:(h + 1) * DV], p0[:, 0:DV],
                ozc[:, h * DV:(h + 1) * DV], ALU.add), deps=[tcr, l1])
            bank0 = [eo]
            if g["name"] == "b":
                sqs = sb["sqscr"][:, (h % 2) * 128:(h % 2) * 128 + 128]
                w1.append(K.s(lambda s, h=h, sqs=sqs: s.activation(
                    out=sqs, in_=otrb[:, h * DV:(h + 1) * DV],
                    func=AF.Square, accum_out=normc[:, h:h + 1]),
                    deps=[eo, w1[-2] if len(w1) >= 2 else None]))
            else:
                w1.append(eo)
        if g["name"] == "b":
            r0 = K.v(lambda v: v.tensor_scalar(normc[:, 0:32], normc[:, 0:32],
                                               1.0 / 128.0, 1e-6, ALU.mult,
                                               ALU.add), deps=w1)
            r1 = K.s(lambda s: s.activation(out=normc[:, 0:32],
                                            in_=normc[:, 0:32], func=AF.Ln,
                                            scale=1.0,
                                            bias=sb["consts"][:, 4:5]), deps=[r0])
            r1b = K.v(lambda v: v.tensor_scalar(normc[:, 0:32], normc[:, 0:32],
                                                -0.5, None, ALU.mult), deps=[r1])
            r2 = K.s(lambda s: s.activation(out=normc[:, 0:32],
                                            in_=normc[:, 0:32], func=AF.Exp),
                     deps=[r1b])
        w2 = []
        bank1 = [gf_dma[-1]] if gf_dma else []
        for h in range(H):
            if g["name"] == "b":
                on = K.v(lambda v, h=h: v.tensor_scalar(
                    sb["otn"][:, 0:128], otrb[:, h * 128:(h + 1) * 128],
                    normc[:, h:h + 1], None, ALU.mult), deps=[r2] + w1)
                tt = K.t(lambda t: t.transpose(p1b[:, 0:128], sb["otn"][:, 0:128],
                                               idb[:, :]), deps=[on] + bank1)
                zs = K.s(lambda s, h=h: s.activation(
                    out=sb["zsil"][:, :], in_=zc[:, h * 128:(h + 1) * 128],
                    func=AF.Silu), deps=[lz, w2[-1] if w2 else None])
                gf = K.v(lambda v, h=h: v.scalar_tensor_tensor(
                    gfc[:, h * 128:(h + 1) * 128], p1b[:, 0:128],
                    sb["bsmall"][:, 2:3], sb["zsil"][:, :], ALU.mult, ALU.mult),
                    deps=[tt, zs])
                bank1 = [gf]
                w2.append(gf)
            else:
                for i in range(2):
                    tt = K.t(lambda t, h=h, i=i: t.transpose(
                        p1b[:, 0:128], otrb[:, h * 256 + i * 128:
                                            h * 256 + (i + 1) * 128], idb[:, :]),
                        deps=w1 + bank1)
                    gf = K.v(lambda v, h=h, i=i: v.tensor_copy(
                        gfc[:, (2 * h + i) * 128:(2 * h + i + 1) * 128],
                        p1b[:, 0:128]), deps=[tt])
                    bank1 = [gf]
                    w2.append(gf)
        gf_dma.append(K.dma(
            "dS", ds["ofin"][:, :, c * 128:(c + 1) * 128].rearrange(
                "t p j -> p t j"),
            gfc.rearrange("p (t j) -> p t j", j=128), deps=w2, ch="gf"))
        ld_free = [gf_dma[-1]] + w2
    owb = sb["regD"][:, 0:1024]
    orhs = sb["regD"][:, 1024:3072]
    res = []
    prev_w = [None, None]
    prev_r = [None, None]
    bankf = {}
    psl = [sb["p%d" % i] for i in range(4)]
    for mg in range(4):
        for half in range(2):
            last_mm = None
            for k in range(32):
                wsl = k % 2
                wt = K.dma("dW", owb[:, wsl * 512:(wsl + 1) * 512],
                           dram[g["WO"]][k * 128:(k + 1) * 128,
                                         mg * 512:(mg + 1) * 512],
                           deps=[prev_w[wsl]], ch="ow%d" % wsl)
                rt = K.dma("dS", orhs[:, wsl * 512:(wsl + 1) * 512],
                           ds["ofin"][k, :, half * 512:(half + 1) * 512],
                           deps=[prev_r[wsl], gf_dma[-1]], ch="or%d" % wsl)
                for m in range(4):
                    extra = [bankf.get(m)] if k == 0 else []
                    last_mm = K.t(lambda t, wsl=wsl, m=m, k=k:
                                  t.matmul(psl[m][:, :],
                                           owb[:, wsl * 512 + m * 128:
                                               wsl * 512 + (m + 1) * 128],
                                           orhs[:, wsl * 512:(wsl + 1) * 512],
                                           start=(k == 0), stop=(k == 31)),
                                  deps=[wt, rt] + [e for e in extra if e])
                prev_w[wsl] = last_mm
                prev_r[wsl] = last_mm
            for m in range(4):
                dm = mg * 4 + m
                xs = sb["xT"][:, dm * T + half * 512: dm * T + (half + 1) * 512]
                e = K.v(lambda v, m=m, xs=xs: v.tensor_tensor(
                    xs, psl[m][:, :], xs, ALU.add), deps=[last_mm])
                bankf[m] = e
                res.append(e)
    return res


def emit_store(K, sb, out, deps):
    toks = []
    for ft in range(NFT):
        toks.append(K.dma("dS", out[ft * 128:(ft + 1) * 128, :],
                          sb["xT"][:, ft * T:(ft + 1) * T], deps=deps,
                          ch="out%d" % (ft % 4)))
    allch = [Tok(c["sem"], c["cnt"]) for c in K.dma_ch.values()]
    K.waiter("sync", allch + toks)
    return toks


def emit_all(K, sb, dram, ds, out):
    dep = emit_init(K, sb, dram)
    if EMIT_ATTN_A:
        dep = emit_mixer(K, sb, dram, ds, dep, GEO_A)
    if EMIT_FFN:
        dep = emit_ffn(K, sb, dram, ds, dep)
    if EMIT_ATTN_B:
        dep = emit_mixer(K, sb, dram, ds, dep, GEO_B)
    if EMIT_FFN:
        dep = emit_ffn(K, sb, dram, ds, dep)
    emit_store(K, sb, out, dep)



_CACHE = {}


def kernel(**inputs):
    per_core = host_prep(inputs)
    if "nc" not in _CACHE:
        _CACHE["nc"] = build_program()
    nc = _CACHE["nc"]
    res = run_bass_kernel_spmd(nc, per_core, core_ids=list(range(8)))
    x = np.asarray(inputs["L0_attn_a_x"])
    out = np.empty_like(x, dtype=np.float32)
    for core in range(8):
        b, h = core // 2, core % 2
        out[b, h * T:(h + 1) * T, :] = res.results[core]["out"].T
    return out


# version 15
# speedup vs baseline: 1.7787x; 1.0263x over previous
import numpy as np
import concourse.bass as bass
import concourse.mybir as mybir
from concourse.bass_utils import run_bass_kernel_spmd

F32 = mybir.dt.float32
BF16 = mybir.dt.bfloat16
AF = mybir.ActivationFunctionType
ALU = mybir.AluOpType

D = 2048
T = 1024
NFT = 16
C = 128
NCH = 8
NEG = -1e30
INTER = 5632
NIT = 44

EMIT_ATTN_A = True
EMIT_ATTN_B = True
EMIT_FFN = True
FFN_STAGE = 3


CC_INSTS = []
DBG_P1 = False


class Tok:
    __slots__ = ("sem", "val")

    def __init__(self, sem, val):
        self.sem = sem
        self.val = val


class RecRes:
    __slots__ = ("ent",)

    def __init__(self, ent):
        self.ent = ent

    def then_inc(self, sem, amt):
        self.ent[3].append((sem, amt))
        return self


class RecEng:
    __slots__ = ("log",)

    def __init__(self, log):
        self.log = log

    def __getattr__(self, name):
        log = self.log

        def call(*a, **kw):
            ent = [name, a, kw, [], None]
            log.append(ent)
            return RecRes(ent)

        return call


class Ker:
    ENGS = ("tensor", "vector", "scalar", "gpsimd", "sync")

    def __init__(self, nc, block, sems):
        self.nc = nc
        self.block = block
        self.st = {}
        for name in self.ENGS:
            self.st[name] = dict(sem=sems[name], cnt=0, amt=1)
        self.dma_pool = sems["dma_pool"]
        self.dma_ch = {}
        self.last_wait = {e: {} for e in self.ENGS}
        self.logs = {e: [] for e in self.ENGS}

    def _ch(self, name):
        if name not in self.dma_ch:
            idx = len(self.dma_ch)
            assert idx < len(self.dma_pool), "out of DMA channel semaphores"
            self.dma_ch[name] = dict(sem=self.dma_pool[idx], cnt=0)
        return self.dma_ch[name]

    def _deps(self, eng_name, deps):
        lw = self.last_wait[eng_name]
        log = self.logs[eng_name]
        for t in deps:
            if t is None:
                continue
            key = t.sem.num if hasattr(t.sem, "num") else id(t.sem)
            if lw.get(key, -1) < t.val:
                log.append(["wait_ge", (t.sem, t.val), {}, [], None])
                lw[key] = t.val

    def _emit(self, stream, eng_name, f, deps, inc=True):
        s = self.st[stream]
        self._deps(eng_name, deps)
        ins = f(RecEng(self.logs[eng_name]))
        if inc:
            ins.then_inc(s["sem"], s["amt"])
            s["cnt"] += s["amt"]
        return Tok(s["sem"], s["cnt"])

    def t(self, f, deps=()):
        return self._emit("tensor", "tensor", f, deps)

    def v(self, f, deps=()):
        return self._emit("vector", "vector", f, deps)

    def s(self, f, deps=()):
        return self._emit("scalar", "scalar", f, deps)

    def g(self, f, deps=()):
        return self._emit("gpsimd", "gpsimd", f, deps)

    def dma(self, q, out, in_, deps=(), ch="x"):
        eng_name = "sync" if q == "dS" else "gpsimd"
        c = self._ch(q + ":" + ch)
        if c["cnt"] > 0:
            self._deps(eng_name, [Tok(c["sem"], c["cnt"])])
        self._deps(eng_name, deps)
        self.logs[eng_name].append(
            ["dma_start", (), dict(out=out, in_=in_), [(c["sem"], 16)], None])
        c["cnt"] += 16
        return Tok(c["sem"], c["cnt"])

    def waiter(self, eng_name, deps):
        self._deps(eng_name, deps)

    def collective(self, kind, op, ins, outs, groups, sem, deps=()):
        s = self.st["gpsimd"]
        self._deps("gpsimd", deps)

        def post(bi):
            CC_INSTS.append(bi.ins if hasattr(bi, "ins") else bi)

        self.logs["gpsimd"].append(
            ["collective_compute", (kind, op),
             dict(ins=ins, outs=outs, replica_groups=groups),
             [(s["sem"], s["amt"])], post])
        s["cnt"] += s["amt"]
        return Tok(s["sem"], s["cnt"])

    def flush(self):
        for eng_name in self.ENGS:
            entries = self.logs[eng_name]
            if not entries:
                continue

            def body(eng, entries=entries):
                for method, a, kw, incs, post in entries:
                    r = getattr(eng, method)(*a, **kw)
                    for sem, amt in incs:
                        r.then_inc(sem, amt)
                    if post is not None:
                        post(r)

            getattr(self.block, eng_name)(body)
            self.logs[eng_name] = []


def _hc(a):
    return np.ascontiguousarray(a)


def host_prep(inputs):
    inp = {k: np.asarray(v) for k, v in inputs.items()}
    x = inp["L0_attn_a_x"]
    shared = {}

    shared["aWqkv"] = _hc(np.concatenate(
        [inp["L0_attn_a_Wq"], inp["L0_attn_a_Wk"], inp["L0_attn_a_Wv"]], axis=1))
    awab = np.zeros((2048, 64), np.float32)
    awab[:, 0:16] = inp["L0_attn_a_Wa"]
    awab[:, 32:48] = inp["L0_attn_a_Wb"]
    shared["aWab"] = _hc(awab)
    shared["aWo"] = _hc(inp["L0_attn_a_Wo"])
    acw = np.concatenate([inp["L0_attn_a_qcw"], inp["L0_attn_a_kcw"],
                          inp["L0_attn_a_vcw"]], axis=0)
    shared["aconvw"] = _hc(acw.reshape(64, 128, 4).transpose(1, 0, 2).reshape(128, 256))
    acb = np.concatenate([inp["L0_attn_a_qcb"], inp["L0_attn_a_kcb"],
                          inp["L0_attn_a_vcb"]], axis=0)
    shared["aconvb"] = _hc(acb.reshape(64, 128).T)

    Wq = inp["L0_attn_b_W_qkvz"].reshape(2048, 16, 768)
    shared["bWqkvz"] = _hc(np.concatenate(
        [Wq[:, :, 0:128].reshape(2048, 2048), Wq[:, :, 128:256].reshape(2048, 2048),
         Wq[:, :, 256:512].reshape(2048, 4096), Wq[:, :, 512:768].reshape(2048, 4096)],
        axis=1))
    Wba = inp["L0_attn_b_W_ba"].reshape(2048, 16, 4)
    shared["bWba"] = _hc(np.concatenate(
        [Wba[:, :, 0:2].reshape(2048, 32), Wba[:, :, 2:4].reshape(2048, 32)], axis=1))
    bcw = inp["L0_attn_b_conv_w"]
    shared["bconvw"] = _hc(bcw.reshape(64, 128, 4).transpose(1, 0, 2).reshape(128, 256))
    shared["bWout"] = _hc(inp["L0_attn_b_W_out"])
    small = np.zeros((128, 8), np.float32)
    small[:32, 0] = inp["L0_attn_b_A_log"]
    small[:32, 1] = inp["L0_attn_b_dt_bias"]
    small[:, 2] = inp["L0_attn_b_norm_w"]
    shared["bsmall"] = _hc(small)

    shared["fgT"] = _hc(inp["L0_ffn_gate_w"].T)
    shared["fuT"] = _hc(inp["L0_ffn_up_w"].T)
    shared["fdT"] = _hc(inp["L0_ffn_down_w"].T)
    shared["fgs"] = _hc(inp["L0_ffn_gate_s"].reshape(44, 128).T)
    shared["fus"] = _hc(inp["L0_ffn_up_s"].reshape(44, 128).T)
    shared["fds"] = _hc(inp["L0_ffn_down_s"].reshape(16, 128).T)

    j = np.arange(128)
    cm = np.zeros((128, 4 * 128), np.float32)
    cm[:, 0:128] = np.where(j[None, :] >= j[:, None], 0.0, NEG)
    cm[:, 128:256] = np.where(j[None, :] > j[:, None], 0.0, NEG)
    cm[:, 256:384] = np.where(j[:, None] > j[None, :], 0.0, NEG)
    cm[:, 384:512] = np.eye(128, dtype=np.float32)
    shared["cmask"] = cm

    per_core = []
    for core in range(8):
        b, h = core // 2, core % 2
        m = dict(shared)
        m["xT"] = _hc(x[b, h * T:(h + 1) * T, :].T)
        fl = np.zeros((128, 8), np.float32)
        fl[:, 0] = 1.0 if h == 1 else 0.0
        m["flags"] = fl
        per_core.append(m)
    return per_core



class ExitStackLite:
    def __init__(self):
        self.cms = []

    def __enter__(self):
        return self

    def enter(self, cm):
        v = cm.__enter__()
        self.cms.append(cm)
        return v

    def __exit__(self, *a):
        for cm in reversed(self.cms):
            cm.__exit__(*a)
        return False


def build_program():
    nc = bass.Bass("TRN2", target_bir_lowering=False)
    dram = {}

    def din(name, shape):
        dram[name] = nc.dram_tensor(name, list(shape), F32, kind="ExternalInput").ap()

    din("xT", (D, T))
    din("aWqkv", (D, 8192)); din("aWab", (D, 64)); din("aWo", (4096, D))
    din("aconvw", (128, 256)); din("aconvb", (128, 64))
    din("bWqkvz", (D, 12288)); din("bWba", (D, 64)); din("bconvw", (128, 256))
    din("bWout", (4096, D)); din("bsmall", (128, 8))
    din("fgT", (D, INTER)); din("fuT", (D, INTER)); din("fdT", (INTER, D))
    din("fgs", (128, 44)); din("fus", (128, 44)); din("fds", (128, 16))
    din("cmask", (128, 4 * 128)); din("flags", (128, 8))

    out = nc.dram_tensor("out", [D, T], F32, kind="ExternalOutput").ap()

    ds = {}
    ds["qkvpre"] = nc.dram_tensor("s_qkvpre", [8192, T + 3], BF16, kind="Internal").ap()
    ds["zraw"] = nc.dram_tensor("s_zraw", [NCH, 128, 4096], BF16, kind="Internal").ap()
    ds["ozero"] = nc.dram_tensor("s_ozero", [NCH, 128, 4096], BF16, kind="Internal").ap()
    ds["util"] = nc.dram_tensor("s_util", [NCH, 128, 4096], BF16, kind="Internal").ap()
    ds["hdn"] = nc.dram_tensor("s_hdn", [NIT, 128, T], BF16, kind="Internal").ap()
    ds["qkvpost"] = nc.dram_tensor("s_qkvpost", [8192, T], BF16, kind="Internal").ap()
    ds["dbg"] = nc.dram_tensor("s_dbg", [16, 128, 512], BF16, kind="Internal").ap()
    ds["ssq"] = nc.dram_tensor("s_ssq", [2, 32, T], F32, kind="Internal").ap()
    ds["gcrow"] = nc.dram_tensor("s_gcrow", [8, 32], F32, kind="Internal").ap()
    ds["krows"] = nc.dram_tensor("s_krows", [96, T], F32, kind="Internal").ap()
    ds["ofin"] = nc.dram_tensor("s_ofin", [32, 128, T], BF16, kind="Internal").ap()
    ds["agh_in"] = nc.dram_tensor("s_agh_in", [128, 192], F32, kind="Internal").ap()
    ds["agh_out"] = nc.dram_tensor("s_agh_out", [256, 192], F32, kind="Internal").ap()
    ds["ags_in"] = nc.dram_tensor("s_ags_in", [128, 4096], BF16, kind="Internal").ap()
    ds["ags_out"] = nc.dram_tensor("s_ags_out", [256, 4096], BF16, kind="Internal").ap()

    with ExitStackLite() as stack:
        sb = {}

        def sbuf(name, shape, dt):
            sb[name] = stack.enter(nc.sbuf_tensor("sb_" + name, list(shape), dt))

        def psum(name, shape):
            sb[name] = stack.enter(nc.psum_tensor("ps_" + name, list(shape), F32))

        sbuf("xT", (128, NFT * T), F32)
        sbuf("regA", (128, 16384), BF16)
        sbuf("regB", (128, 16384), BF16)
        sbuf("regC", (128, 4096), BF16)
        sbuf("regD", (128, 3072), BF16)
        sbuf("regE", (128, 2048), BF16)
        sbuf("regF", (128, 1024), F32)
        sbuf("rows", (128, 2048), F32)
        sbuf("scales", (128, 128), F32)
        sbuf("cmask", (128, 4 * 128), F32)
        sbuf("consts", (128, 16), F32)
        sbuf("onesr", (1, 128), F32)
        sbuf("onesb", (128, 8), BF16)
        sbuf("idb", (128, 128), BF16)
        sbuf("flags", (128, 8), F32)
        sbuf("kTf", (128, NCH * 256), F32)
        sbuf("k2d", (128, 1024), F32)
        sbuf("k2d2", (128, 1024), F32)
        sbuf("gates", (64, 1024), F32)
        sbuf("expgc", (128, NCH * 32), F32)
        sbuf("halocol", (128, 192), F32)
        sbuf("hrecv", (128, 192), F32)
        sbuf("halob", (128, 192), BF16)
        sbuf("convw", (128, 256), F32)
        sbuf("convb", (128, 64), F32)
        sbuf("ATd", (128, 256), BF16)
        sbuf("NTn", (128, 256), BF16)
        sbuf("Nn", (128, 256), BF16)
        sbuf("sq", (128, 512), BF16)
        sbuf("sq2", (128, 512), BF16)
        sbuf("xb", (128, 768), BF16)
        sbuf("xb2", (128, 768), BF16)
        sbuf("ktok", (128, 256), BF16)
        sbuf("u0t", (128, 256), BF16)
        sbuf("qpf", (128, 256), BF16)
        sbuf("pp", (128, 512), BF16)
        sbuf("ifacbc", (128, 256), BF16)
        sbuf("regG", (128, 1024), F32)
        sbuf("otn", (128, 256), BF16)
        sbuf("zsil", (128, 128), BF16)
        sbuf("normc", (128, 64), F32)
        sbuf("sqscr", (128, 512), BF16)
        sbuf("bsmall", (128, 8), F32)
        psum("p0", (128, 512)); psum("p1", (128, 512))
        psum("p2", (128, 512)); psum("p3", (128, 512))
        psum("p4", (128, 512)); psum("p5", (128, 512))
        psum("p6", (128, 512))
        sb["p7"] = stack.enter(nc.psum_tensor("ps_p7", [128, 1024], BF16))

        sems = {}
        for name in ("tensor", "vector", "scalar", "gpsimd", "sync", "cc"):
            sems[name] = stack.enter(nc.semaphore(name))
        sems["dma_pool"] = [stack.enter(nc.semaphore("dmach%d" % i))
                            for i in range(64)]

        with nc.Block() as block:
            K = Ker(nc, block, sems)
            emit_all(K, sb, dram, ds, out)
            K.flush()

    return nc



def emit_init(K, sb, dram):
    toks = [
        K.dma("dS", sb["cmask"][:, :], dram["cmask"][:, :], ch="init"),
        K.dma("dS", sb["flags"][:, :], dram["flags"][:, :], ch="init"),
    ]
    for ft in range(NFT):
        toks.append(K.dma("dS", sb["xT"][:, ft * T:(ft + 1) * T],
                          dram["xT"][ft * 128:(ft + 1) * 128, :], ch="xt%d" % (ft % 4)))
    ld = list(toks)
    toks.append(K.v(lambda v: v.memset(sb["consts"][:, 0:1], 1.0)))
    toks.append(K.v(lambda v: v.memset(sb["consts"][:, 1:2], 1e-5)))
    toks.append(K.v(lambda v: v.memset(sb["consts"][:, 2:3], 1e-6)))
    toks.append(K.v(lambda v: v.memset(sb["consts"][:, 3:4], 1e-18)))
    toks.append(K.v(lambda v: v.memset(sb["consts"][:, 4:5], 0.0)))
    toks.append(K.dma("dS", sb["bsmall"][:, :], dram["bsmall"][:, :], ch="init"))
    toks.append(K.v(lambda v: v.memset(sb["onesb"][:, :], 1.0)))
    toks.append(K.v(lambda v: v.memset(sb["onesr"][:, :], 1.0)))
    toks.append(K.v(lambda v: v.tensor_copy(sb["idb"][:, :], sb["cmask"][:, 384:512]),
                    deps=ld))
    return toks


def emit_ln(K, sb, dep_x):
    xT, rows, lnx = sb["xT"], sb["rows"], sb["regA"]
    ones = sb["consts"][:, 0:1]
    eps = sb["consts"][0:1, 1:2]
    outs = []
    bankfree = None
    for half in range(2):
        t0 = half * 512
        p_s, p_q = sb["p6"], sb["p5"]
        mu = rows[0:1, 0:512]
        var = rows[0:1, 512:1024]
        rinv = rows[0:1, 1024:1536]
        cc = rows[0:1, 1536:2048]
        tsum = None
        sq_toks = []
        for ft in range(NFT):
            xs = xT[:, ft * T + t0: ft * T + t0 + 512]
            sq = sb["regF"][:, (ft % 2) * 512:(ft % 2) * 512 + 512]
            prev_mm = sq_toks[-1] if sq_toks else None
            dsq = K.v(lambda v, xs=xs, sq=sq: v.tensor_tensor(sq, xs, xs, ALU.mult),
                      deps=list(dep_x) + ([prev_mm] if prev_mm else []))
            tsum = K.t(lambda t, xs=xs, ft=ft: t.matmul(
                p_s[0:1, :], ones, xs, start=(ft == 0), stop=(ft == NFT - 1)),
                deps=list(dep_x) + ([bankfree] if (ft == 0 and bankfree) else []))
            tq = K.t(lambda t, sq=sq, ft=ft: t.matmul(
                p_q[0:1, :], ones, sq, start=(ft == 0), stop=(ft == NFT - 1)),
                deps=[dsq])
            sq_toks.append(tq)
        tq_last = sq_toks[-1]
        d1 = K.v(lambda v: v.tensor_scalar(mu, p_s[0:1, :], 1.0 / D, None, ALU.mult),
                 deps=[tsum, tq_last])
        d2 = K.v(lambda v: v.tensor_scalar(var, p_q[0:1, :], 1.0 / D, None, ALU.mult),
                 deps=[tsum, tq_last])
        d3 = K.v(lambda v: v.tensor_tensor(rinv, mu, mu, ALU.mult), deps=[d1])
        d4 = K.v(lambda v: v.tensor_tensor(var, var, rinv, ALU.subtract), deps=[d2, d3])
        d5 = K.s(lambda s: s.activation(out=var, in_=var, func=AF.Ln, scale=1.0,
                                        bias=eps), deps=[d4])
        d5b = K.v(lambda v: v.tensor_scalar(var, var, -0.5, None, ALU.mult),
                  deps=[d5])
        d6 = K.s(lambda s: s.activation(out=rinv, in_=var, func=AF.Exp),
                 deps=[d5b])
        d7 = K.v(lambda v: v.scalar_tensor_tensor(cc, mu, -1.0, rinv, ALU.mult,
                                                  ALU.mult), deps=[d6])
        onesr = sb["onesr"][:, :]
        br = K.t(lambda t: t.matmul(p_s[:, :], onesr, rinv, start=True, stop=True),
                 deps=[d6])
        bc = K.t(lambda t: t.matmul(p_q[:, :], onesr, cc, start=True, stop=True),
                 deps=[d7])
        for ft in range(NFT):
            xs = xT[:, ft * T + t0: ft * T + t0 + 512]
            dst = lnx[:, ft * T + t0: ft * T + t0 + 512]
            tmp = sb["regF"][:, (ft % 2) * 512:(ft % 2) * 512 + 512]
            a1 = K.v(lambda v, xs=xs, tmp=tmp: v.tensor_tensor(tmp, xs, p_s[:, :],
                                                               ALU.mult), deps=[br])
            a2 = K.v(lambda v, tmp=tmp, dst=dst: v.tensor_tensor(dst, tmp, p_q[:, :],
                                                                 ALU.add), deps=[bc, a1])
            outs.append(a2)
        bankfree = outs[-1]
    return outs


def emit_ffn(K, sb, dram, ds, dep_in):
    ln_done = emit_ln(K, sb, dep_in)
    if FFN_STAGE == 1:
        return ln_done
    sc = [K.dma("dS", sb["scales"][:, 0:44], dram["fgs"][:, :], ch="init"),
          K.dma("dS", sb["scales"][:, 44:88], dram["fus"][:, :], ch="init"),
          K.dma("dS", sb["scales"][:, 88:104], dram["fds"][:, :], ch="init")]
    lnx = sb["regA"]
    slab_tok = {}
    prev_use = {}
    hdn_toks = []
    ev_prev = {}
    wg = sb["regB"][:, 0:8192]
    wu = sb["regB"][:, 8192:16384]
    wd = sb["regE"]
    hdnb = sb["regD"][:, 0:2048]
    hdnc2 = [sb["regC"][:, 0:1024], sb["regC"][:, 1024:2048]]
    for m in range(NIT):
        mg, mloc = m // 2, m % 2
        slot = mg % 2
        if mloc == 0:
            depw = [prev_use.get(("g", slot)), prev_use.get(("u", slot))]
            src = dram["fgT"][:, mg * 256:(mg + 1) * 256].rearrange(
                "(k p) c -> p k c", p=128)
            dst = wg[:, slot * 4096:(slot + 1) * 4096].rearrange(
                "p (k c) -> p k c", k=16)
            slab_tok[("g", mg)] = K.dma("dW", dst, src, deps=[d for d in depw if d],
                                        ch="wg%d" % slot)
            src = dram["fuT"][:, mg * 256:(mg + 1) * 256].rearrange(
                "(k p) c -> p k c", p=128)
            dst = wu[:, slot * 4096:(slot + 1) * 4096].rearrange(
                "p (k c) -> p k c", k=16)
            slab_tok[("u", mg)] = K.dma("dW", dst, src, deps=[d for d in depw if d],
                                        ch="wu%d" % slot)
        tg = tu = None
        for half in range(2):
            pg = sb["p0"] if half == 0 else sb["p1"]
            pu = sb["p2"] if half == 0 else sb["p3"]
            for k in range(16):
                lg = wg[:, slot * 4096 + k * 256 + mloc * 128:
                        slot * 4096 + k * 256 + mloc * 128 + 128]
                lu = wu[:, slot * 4096 + k * 256 + mloc * 128:
                        slot * 4096 + k * 256 + mloc * 128 + 128]
                rhs = lnx[:, k * T + half * 512: k * T + half * 512 + 512]
                extra = []
                if k == 0:
                    extra = ln_done + [slab_tok[("g", mg)], ev_prev.get(("g", half))]
                tg = K.t(lambda t, lg=lg, rhs=rhs, pg=pg, k=k: t.matmul(
                    pg[:, :], lg, rhs, start=(k == 0), stop=(k == 15)),
                    deps=[slab_tok[("g", mg)]] + [e for e in extra if e])
                extra = []
                if k == 0:
                    extra = ln_done + [slab_tok[("u", mg)], ev_prev.get(("u", half))]
                tu = K.t(lambda t, lu=lu, rhs=rhs, pu=pu, k=k: t.matmul(
                    pu[:, :], lu, rhs, start=(k == 0), stop=(k == 15)),
                    deps=[slab_tok[("u", mg)]] + [e for e in extra if e])
            gs = sb["scales"][:, m:m + 1]
            us = sb["scales"][:, 44 + m:44 + m + 1]
            sg = sb["regF"][:, half * 512: half * 512 + 512]
            e1 = K.s(lambda s, pg=pg, sg=sg, gs=gs: s.activation(
                out=sg, in_=pg[:, :], func=AF.Silu, scale=gs),
                deps=[tg, tu] + sc + [ev_prev.get(("u", half))])
            dst = hdnc2[m % 2][:, half * 512: half * 512 + 512]
            e2 = K.v(lambda v, pu=pu, us=us, sg=sg, dst=dst: v.scalar_tensor_tensor(
                dst, pu[:, :], us, sg, ALU.mult, ALU.mult),
                deps=[e1, hdn_toks[m - 2] if m >= 2 else None])
            ev_prev[("g", half)] = e1
            ev_prev[("u", half)] = e2
        prev_use[("g", slot)] = tg
        prev_use[("u", slot)] = tu
        K.t(lambda t: t.ldweights(sb["idb"][:, :]), deps=[])
        hdn_toks.append(K.dma("dS", ds["hdn"][m, :, :], hdnc2[m % 2][:, :],
                              deps=[ev_prev[("g", 0)], ev_prev[("u", 0)],
                                    ev_prev[("g", 1)], ev_prev[("u", 1)]],
                              ch="hdnst%d" % (m % 2)))
    if FFN_STAGE == 2:
        return hdn_toks
    res = []
    hprev = {}
    dprev = {}
    bank_free = {}
    psl = [sb["p%d" % i] for i in range(4)]
    guard0 = [ev_prev[("g", 0)], ev_prev[("u", 0)],
              ev_prev[("g", 1)], ev_prev[("u", 1)]]
    for grp in range(4):
        for half in range(2):
            for i in range(NIT):
                hslot = i % 4
                hb = hdnb[:, hslot * 512: hslot * 512 + 512]
                dep_h = [hprev.get(hslot), hdn_toks[i]]
                ht = K.dma("dS", hb,
                           ds["hdn"][i, :, half * 512:(half + 1) * 512],
                           deps=[d for d in dep_h if d],
                           ch="hdnb%d" % hslot)
                wslot = i % 4
                src = dram["fdT"][i * 128:(i + 1) * 128, grp * 512:(grp + 1) * 512]
                wt = K.dma("dW", wd[:, wslot * 512:(wslot + 1) * 512], src,
                           deps=[d for d in [dprev.get(wslot)] if d],
                           ch="wd%d" % wslot)
                last_mm = None
                for dmi in range(4):
                    lw = wd[:, wslot * 512 + dmi * 128:
                            wslot * 512 + dmi * 128 + 128]
                    extra = []
                    if i == 0:
                        extra = [bank_free.get(dmi)] if bank_free else guard0
                    last_mm = K.t(lambda t, lw=lw, hb=hb, p=psl[dmi], i=i: t.matmul(
                        p[:, :], lw, hb, start=(i == 0), stop=(i == NIT - 1)),
                        deps=[ht, wt] + [e for e in extra if e])
                hprev[hslot] = last_mm
                dprev[wslot] = last_mm
                K.t(lambda t: t.ldweights(sb["idb"][:, :]), deps=[])
            for dmi in range(4):
                dm = grp * 4 + dmi
                dsc = sb["scales"][:, 88 + dm:88 + dm + 1]
                xs = sb["xT"][:, dm * T + half * 512: dm * T + half * 512 + 512]
                e = K.v(lambda v, p=psl[dmi], dsc=dsc, xs=xs: v.scalar_tensor_tensor(
                    xs, p[:, :], dsc, xs, ALU.mult, ALU.add), deps=[last_mm])
                bank_free[dmi] = e
                res.append(e)
    return res




GEO_A = dict(name="a", H=16, KH=16, R=1, DK=128, DV=256, NM=64,
             WQKV="aWqkv", WAB="aWab", WO="aWo", CW="aconvw", CB="aconvb")
GEO_B = dict(name="b", H=32, KH=16, R=2, DK=128, DV=128, NM=96,
             WQKV="bWqkvz", WAB="bWba", WO="bWout", CW="bconvw", CB=None)

CC_GROUPS = [[0, 1], [2, 3], [4, 5], [6, 7]]
MIXER_STAGE = 9


def emit_mixer(K, sb, dram, ds, dep_in, g):
    ln_done = emit_ln(K, sb, dep_in)
    lnx = sb["regA"]

    gw = sb["regB"][:, 6144:7168]
    src = dram[g["WAB"]][:, :].rearrange("(k p) c -> p k c", p=128)
    gwt = K.dma("dW", gw.rearrange("p (k c) -> p k c", k=16), src, ch="gw")
    gtoks = []
    for half in range(2):
        tg = None
        for k in range(16):
            rhs = lnx[:, k * T + half * 512: k * T + half * 512 + 512]
            tg = K.t(lambda t, k=k, rhs=rhs: t.matmul(
                sb["p6"][0:64, :], gw[:, k * 64:(k + 1) * 64], rhs,
                start=(k == 0), stop=(k == 15)),
                deps=[gwt] + (ln_done if k == 0 else []) + (gtoks if k == 0 else []))
        gtoks.append(K.v(lambda v, half=half: v.tensor_copy(
            sb["gates"][:, half * 512: half * 512 + 512], sb["p6"][0:64, :]),
            deps=[tg]))

    slabs = [sb["regB"][:, 0:2048], sb["regB"][:, 2048:4096],
             sb["regB"][:, 7168:9216], sb["regB"][:, 9216:11264]]
    stg = sb["regB"][:, 4096:6144]
    prev_mm = [None, None, None, None]
    stg_free = [None] * 4
    bankq = {i: [] for i in range(4)}
    halo_toks = []
    pj_done = []
    for m in range(g["NM"]):
        slot = m % 4
        wsl = slabs[slot]
        src = dram[g["WQKV"]][:, m * 128:(m + 1) * 128].rearrange(
            "(k p) c -> p k c", p=128)
        wt = K.dma("dW", wsl.rearrange("p (k c) -> p k c", k=16), src,
                   deps=[prev_mm[slot]], ch="wq%d" % slot)
        for half in range(2):
            pq = sb["p%d" % (2 * (m % 2) + half)]
            tg = None
            for k in range(16):
                rhs = lnx[:, k * T + half * 512: k * T + half * 512 + 512]
                tg = K.t(lambda t, k=k, rhs=rhs, pq=pq, wsl=wsl: t.matmul(
                    pq[:, :], wsl[:, k * 128:(k + 1) * 128], rhs,
                    start=(k == 0), stop=(k == 15)),
                    deps=[wt] + (ln_done if k == 0 else []) +
                         (bankq[2 * (m % 2) + half] if k == 0 else []) +
                         ([stg_free[2 * (m % 2) + half]] if k == 0 else []))
            prev_mm[slot] = tg
            ssl = 2 * (m % 2) + half
            sstg = stg[:, ssl * 512:(ssl + 1) * 512]
            e = K.s(lambda s, pq=pq, sstg=sstg: s.activation(
                out=sstg, in_=pq[:, :], func=AF.Copy),
                deps=[tg, stg_free[ssl]])
            bankq[ssl] = [e]
            if m < 64:
                dst = ds["qkvpre"][m * 128:(m + 1) * 128,
                                   3 + half * 512: 3 + half * 512 + 512]
                dt_ = K.dma("dS", dst, sstg, deps=[e], ch="qst%d" % ssl)
            else:
                zt = m - 64
                dst = ds["zraw"].rearrange("c p (zt j) -> p zt c j", j=128)[
                    :, zt, half * 4:(half + 1) * 4, :]
                dt_ = K.dma("dS", dst, sstg.rearrange("p (c j) -> p c j", j=128),
                            deps=[e], ch="qst%d" % ssl)
            stg_free[ssl] = dt_
            pj_done.append(dt_)
            K.t(lambda t: t.ldweights(sb["idb"][:, :]), deps=[])
            if m < 64 and half == 1:
                halo_toks.append(K.v(lambda v, pq=pq, m=m: v.tensor_copy(
                    sb["halocol"][:, m * 3:(m + 1) * 3], pq[:, 509:512]),
                    deps=[tg]))
                bankq[ssl] = bankq[ssl] + [halo_toks[-1]]
    d1 = K.dma("dS", ds["agh_in"][:, :], sb["halocol"][:, :], deps=halo_toks,
               ch="ag")
    cc1 = K.collective("AllGather", ALU.bypass, [ds["agh_in"][:, :]],
                       [ds["agh_out"][:, :]], CC_GROUPS, None, deps=[d1])
    d2 = K.dma("dS", sb["hrecv"][:, :], ds["agh_out"][0:128, :], deps=[cc1],
               ch="ag")
    hb = K.v(lambda v: v.tensor_scalar(sb["halob"][:, :], sb["hrecv"][:, :],
                                       sb["flags"][:, 0:1], None, ALU.mult),
             deps=[d2])

    cwt = K.dma("dS", sb["convw"][:, :], dram[g["CW"]][:, :], ch="cw")
    cbt = None
    if g["CB"]:
        cbt = K.dma("dS", sb["convb"][:, :], dram[g["CB"]][:, :], ch="cw")
    pre = sb["regB"][:, 0:8216]
    cdiag = sb["regE"]
    cstg = sb["regB"][:, 12288:16384]
    pre_free = [None, None]
    cd_free = [None, None]
    cstg_free = [None, None]
    pc_free = [None, None]
    ssq_evac = None
    ssq_dma = []
    qk_done = []
    for grp in range(16):
        gslot = grp % 2
        psrc = ds["qkvpre"][grp * 512:(grp + 1) * 512, 3:].rearrange(
            "(t p) c -> p t c", p=128)
        pdep = [pre_free[gslot], hb] + (list(stg_free) if grp == 0 else [])
        pt = K.dma("dS", pre[:, gslot * 4108:(gslot + 1) * 4108].rearrange(
            "p (t c) -> p t c", t=4)[:, :, 3:], psrc,
            deps=[d for d in pdep if d], ch="pre%d" % gslot)
        hm = K.v(lambda v, gslot=gslot, grp=grp: v.tensor_copy(
            pre[:, gslot * 4108:(gslot + 1) * 4108].rearrange(
                "p (t c) -> p t c", t=4)[:, :, 0:3],
            sb["halob"][:, grp * 12:(grp + 1) * 12].rearrange(
                "p (t c) -> p t c", t=4)),
            deps=[pt, hb])
        for tl in range(4):
            t_ = grp * 4 + tl
            dslot = t_ % 2
            cslot = t_ % 2
            dtk = []
            for tap in range(4):
                dtk.append(K.v(lambda v, dslot=dslot, tap=tap, t_=t_: v.tensor_scalar(
                    cdiag[:, dslot * 512 + tap * 128: dslot * 512 + (tap + 1) * 128],
                    sb["cmask"][:, 384:512],
                    sb["convw"][:, t_ * 4 + tap:t_ * 4 + tap + 1],
                    None, ALU.mult), deps=[cwt, cd_free[dslot]]))
            last_sil = None
            sils = []
            for pan in range(4):
                pslot = (t_ * 4 + pan) % 2
                pc = sb["p%d" % (4 + pslot)]
                tm = None
                for tap in range(4):
                    rhs = pre[:, gslot * 4108 + tl * 1027 + pan * 256 + tap:
                              gslot * 4108 + tl * 1027 + pan * 256 + tap + 256]
                    tm = K.t(lambda t, pc=pc, dslot=dslot, tap=tap, rhs=rhs: t.matmul(
                        pc[:, 0:256], cdiag[:, dslot * 512 + tap * 128:
                                            dslot * 512 + (tap + 1) * 128], rhs,
                        start=(tap == 0), stop=(tap == 3)),
                        deps=dtk + [hm, pc_free[pslot]])
                cd_free[dslot] = tm
                dst = cstg[:, cslot * 2048 + pan * 256: cslot * 2048 + (pan + 1) * 256]
                if g["CB"]:
                    e = K.s(lambda s, pc=pc, dst=dst, t_=t_: s.activation(
                        out=dst, in_=pc[:, 0:256], func=AF.Silu,
                        bias=sb["convb"][:, t_:t_ + 1]),
                        deps=[tm, cbt, cstg_free[cslot]])
                else:
                    e = K.s(lambda s, pc=pc, dst=dst: s.activation(
                        out=dst, in_=pc[:, 0:256], func=AF.Silu),
                        deps=[tm, cstg_free[cslot]])
                pc_free[pslot] = e
                last_sil = e
                sils.append(e)
                K.t(lambda t: t.ldweights(sb["idb"][:, :]), deps=[])
            if t_ < 32:
                evs = []
                for pair in range(2):
                    sqs = []
                    for j in range(2):
                        pan = pair * 2 + j
                        dst = cstg[:, cslot * 2048 + pan * 256:
                                   cslot * 2048 + (pan + 1) * 256]
                        sqs.append(K.v(lambda v, dst=dst, j=j: v.tensor_mul(
                            sb["sqscr"][:, j * 256:(j + 1) * 256], dst, dst),
                            deps=[sils[pan], ssq_evac]))
                    tq = K.t(lambda t: t.matmul(
                        sb["p6"][0:1, 0:512], sb["onesb"][:, 0:1],
                        sb["sqscr"][:, 0:512], start=True, stop=True),
                        deps=sqs + evs)
                    ssq_evac = tq
                    evs.append(K.v(lambda v, pair=pair: v.tensor_copy(
                        sb["rows"][0:1, pair * 512:(pair + 1) * 512],
                        sb["p6"][0:1, :]), deps=[tq]))
                qk = 0 if t_ < 16 else 1
                qkix = t_ % 16
                if g["R"] == 1:
                    ssq_dma.append(K.dma(
                        "dS", ds["ssq"][qk, qkix, :],
                        sb["rows"][0:1, 0:1024], deps=evs, ch="ssq"))
                else:
                    ssq_dma.append(K.dma(
                        "dS", ds["ssq"][qk, 2 * qkix, :][None, :],
                        sb["rows"][0:1, 0:1024], deps=evs, ch="ssq"))
                    ssq_dma.append(K.dma(
                        "dS", ds["ssq"][qk, 2 * qkix + 1, :][None, :],
                        sb["rows"][0:1, 0:1024], deps=evs, ch="ssq"))
                ssq_evac = ssq_dma[-1]
            qk_done.append(K.dma("dS", ds["qkvpost"][t_ * 128:(t_ + 1) * 128, :],
                                 cstg[:, cslot * 2048: cslot * 2048 + 1024],
                                 deps=[last_sil], ch="cst%d" % cslot))
            cstg_free[cslot] = qk_done[-1]
            if tl == 3:
                pre_free[gslot] = tm

    if MIXER_STAGE <= 1:
        return qk_done + pj_done + gtoks

    kd = emit_kinds(K, sb, dram, ds, g, gtoks, ssq_dma)
    if MIXER_STAGE <= 2:
        return qk_done + [kd["ktf"][-1], kd["egc"][-1]]

    p1 = emit_pass1(K, sb, dram, ds, g, qk_done, kd)
    if MIXER_STAGE <= 3:
        return p1["all"]

    szb = sb["regB"][:, 0:4096]
    d3 = K.dma("dS", ds["ags_in"][:, :], szb, deps=p1["chain"], ch="ag")
    cc2 = K.collective("AllGather", ALU.bypass, [ds["ags_in"][:, :]],
                       [ds["ags_out"][:, :]], CC_GROUPS, None, deps=[d3])
    d4 = K.dma("dS", szb, ds["ags_out"][0:128, :], deps=[cc2] + p1["all"], ch="ag")
    sin_tok = K.v(lambda v: v.tensor_scalar(sb["regC"][:, :], szb,
                                            sb["flags"][:, 0:1], None, ALU.mult),
                  deps=[d4])

    return emit_pass2(K, sb, dram, ds, g, sin_tok, p1)




def emit_softplus(K, sb, out_ap, in_ap, scale, bias_ap, scr, deps):
    hs = out_ap.shape[0]
    ones_col = sb["consts"][96:96 + hs, 0:1]
    scrB = sb["rows"][96:96 + hs, 0:1024]
    if bias_ap is None:
        a0 = K.v(lambda v: v.tensor_scalar(scr, in_ap, scale, None, ALU.mult),
                 deps=deps)
    else:
        a0 = K.v(lambda v: v.tensor_scalar(scr, in_ap, bias_ap, None, ALU.add),
                 deps=deps)
        if scale != 1.0:
            a0 = K.v(lambda v: v.tensor_scalar(scr, scr, scale, None, ALU.mult),
                     deps=[a0])
    a1 = K.v(lambda v: v.scalar_tensor_tensor(scrB, scr, -1.0, scr, ALU.mult,
                                              ALU.min), deps=[a0])
    a2 = K.s(lambda s: s.activation(out=scrB, in_=scrB, func=AF.Exp), deps=[a1])
    a3 = K.s(lambda s: s.activation(out=scrB, in_=scrB, func=AF.Ln, scale=1.0,
                                    bias=ones_col), deps=[a2])
    a4 = K.v(lambda v: v.tensor_scalar_max(scr, scr, 0.0), deps=[a1])
    return K.v(lambda v: v.tensor_add(out_ap, scr, scrB), deps=[a3, a4])


def emit_kinds(K, sb, dram, ds, g, gtoks, ssq_dma):
    k2d, k2d2 = sb["k2d"], sb["k2d2"]
    gates = sb["gates"]
    hs = g["H"]
    A_ = k2d[0:hs, :]
    B_ = k2d[32:32 + hs, :]
    C_ = k2d[64:64 + hs, :]
    NG = k2d[96:96 + hs, :]
    F_ = k2d2[0:hs, :]
    E_ = k2d2[32:32 + hs, :]
    D_ = k2d2[64:64 + hs, :]
    SP = k2d2[96:96 + hs, :]
    scr96 = sb["regF"][96:96 + hs, 0:1024]
    nG0 = sb["regF"][0:hs, 0:1024]
    nG32 = sb["regF"][32:32 + hs, 0:1024]
    nG64 = sb["regF"][64:64 + hs, 0:1024]
    mz1 = K.v(lambda v: v.memset(k2d[:, :], 0.0), deps=gtoks)
    mz2 = K.v(lambda v: v.memset(k2d2[:, :], 0.0), deps=gtoks)
    gtoks = gtoks + [mz1, mz2]
    if g["name"] == "a":
        apre, bpre = gates[0:16, :], gates[32:48, :]
        sp = emit_softplus(K, sb, SP, apre, -1.0, None, scr96, gtoks)
        lnq_c, lnk_c = 0.0, 0.0
        ln_scale = 1.0
        eps_col = 3
    else:
        bpre, apre = gates[0:32, :], gates[32:64, :]
        mexpA = K.s(lambda s: s.activation(out=sb["normc"][96:128, 0:1],
                                           in_=sb["bsmall"][0:32, 0:1],
                                           func=AF.Exp), deps=gtoks)
        dtc = K.v(lambda v: v.tensor_copy(sb["normc"][96:128, 1:2],
                                          sb["bsmall"][0:32, 1:2]), deps=gtoks)
        sp0 = emit_softplus(K, sb, SP, apre, 1.0, sb["normc"][96:128, 1:2],
                            scr96, gtoks + [dtc])
        sp = K.v(lambda v: v.tensor_scalar(SP, SP, sb["normc"][96:128, 0:1],
                                           None, ALU.mult), deps=[sp0, mexpA])
        lnq_c, lnk_c = float(-np.log(128.0)), float(-0.5 * np.log(128.0))
        ln_scale = 1.0 / 128.0
    ngt = []
    for c in range(NCH):
        ngt.append(K.v(lambda v, c=c: v.tensor_tensor_scan(
            NG[:, c * 128:(c + 1) * 128], SP[:, c * 128:(c + 1) * 128],
            SP[:, c * 128:(c + 1) * 128], 0.0, ALU.add, ALU.bypass), deps=[sp]))
    ng_c = [K.v(lambda v: v.tensor_copy(nG0, NG), deps=ngt),
            K.v(lambda v: v.tensor_copy(nG32, NG), deps=ngt),
            K.v(lambda v: v.tensor_copy(nG64, NG), deps=ngt)]
    spb = emit_softplus(K, sb, SP, bpre, -1.0, None, scr96, gtoks + ngt + ng_c)
    sg32 = sb["rows"][32:32 + hs, 0:1024]
    bt0 = K.v(lambda v: v.tensor_scalar(sg32, bpre, -1.0, None, ALU.mult),
              deps=gtoks)
    bt1 = K.s(lambda s: s.activation(out=sg32, in_=sg32, func=AF.Exp), deps=[bt0])
    bt2 = K.v(lambda v: v.tensor_scalar(sg32, sg32, 1.0, None, ALU.add), deps=[bt1])
    beta = K.v(lambda v: v.reciprocal(E_, sg32), deps=[bt2])

    def lnr_load(dst, qk, dep_extra):
        if g["name"] == "a":
            t0 = K.dma("dS", dst, ds["ssq"][qk, 0:16, :],
                       deps=ssq_dma + gtoks + dep_extra, ch="lnr")
            t1 = K.v(lambda v: v.tensor_scalar_max(dst, dst, 1e-18), deps=[t0])
        else:
            t1 = K.dma("dS", dst, ds["ssq"][qk, 0:32, :],
                       deps=ssq_dma + gtoks + dep_extra, ch="lnr")
        return t1

    def lnr_ln(dst, base, cnst, dep):
        if g["name"] == "b":
            dep = K.v(lambda v: v.tensor_scalar(dst, dst, 1.0 / 128.0, 1e-6,
                                                ALU.mult, ALU.add), deps=[dep])
        l_ = K.s(lambda s: s.activation(out=dst, in_=dst, func=AF.Ln,
                                        scale=1.0,
                                        bias=sb["consts"][base:base + hs, 4:5]),
                 deps=[dep])
        return K.v(lambda v: v.tensor_scalar(dst, dst, -0.5, cnst, ALU.mult,
                                             ALU.add), deps=[l_])

    lq = lnr_load(A_, 0, [])
    hq = lnr_ln(A_, 0, lnq_c, lq)
    a_ = K.v(lambda v: v.scalar_tensor_tensor(A_, nG0, -1.0, A_, ALU.mult,
                                              ALU.add), deps=[hq] + ng_c)
    lk = lnr_load(C_, 1, [])
    hk = lnr_ln(C_, 64, lnk_c, lk)
    c_ = K.v(lambda v: v.scalar_tensor_tensor(C_, nG64, 1.0, C_, ALU.mult,
                                              ALU.add), deps=[hk] + ng_c)
    lk2 = lnr_load(B_, 1, [])
    hk2 = lnr_ln(B_, 32, lnk_c, lk2)
    b1 = K.v(lambda v: v.scalar_tensor_tensor(B_, nG32, -1.0, B_, ALU.mult,
                                              ALU.add), deps=[hk2] + ng_c)
    sc32 = K.v(lambda v: v.tensor_copy(nG32, SP), deps=[b1, spb])
    b_ = K.v(lambda v: v.tensor_sub(B_, B_, nG32), deps=[sc32])
    f1 = K.s(lambda s: s.activation(out=F_, in_=B_, func=AF.Exp), deps=[b_])
    f_ = K.v(lambda v: v.tensor_scalar(F_, F_, -1.0, None, ALU.mult), deps=[f1])
    dts = []
    for c in range(NCH):
        dts.append(K.v(lambda v, c=c: v.tensor_scalar(
            D_[:, c * 128:(c + 1) * 128], C_[:, c * 128:(c + 1) * 128],
            nG64[:, c * 128 + 127:c * 128 + 128], None, ALU.subtract),
            deps=[c_] + ng_c))
    dex = K.s(lambda s: s.activation(out=D_, in_=D_, func=AF.Exp), deps=dts)
    gc1 = K.v(lambda v: v.tensor_copy(
        sb["normc"][0:hs, 8:16], NG[:, 127:1024:128]), deps=ngt + ssq_dma)
    gc1b = K.v(lambda v: v.tensor_scalar(sb["normc"][0:hs, 8:16],
                                         sb["normc"][0:hs, 8:16], -1.0, None,
                                         ALU.mult), deps=[gc1])
    gc2 = K.s(lambda s: s.activation(out=sb["normc"][0:hs, 8:16],
                                     in_=sb["normc"][0:hs, 8:16], func=AF.Exp),
              deps=[gc1b])
    gc3 = K.t(lambda t: t.transpose(sb["p6"][0:8, 0:hs], sb["normc"][0:hs, 8:16],
                                    sb["cmask"][0:hs, 384:384 + hs]),
              deps=[gc2] + ssq_dma)
    gc4 = K.v(lambda v: v.tensor_copy(sb["normc"][0:8, 16:16 + hs],
                                      sb["p6"][0:8, 0:hs]), deps=[gc3])
    gc5 = K.dma("dS", ds["gcrow"][:, 0:hs], sb["normc"][0:8, 16:16 + hs],
                deps=[gc4], ch="gc")
    gc6 = K.dma("dS", sb["rows"][0:1, 1024:1024 + 8 * hs].rearrange(
                    "p (a b) -> p a b", a=8),
                ds["gcrow"][None, :, 0:hs], deps=[gc5], ch="gc")
    egc = []
    for c in range(NCH):
        tb = K.t(lambda t, c=c: t.matmul(
            sb["p6"][:, 0:hs], sb["onesr"][:, :],
            sb["rows"][0:1, 1024 + c * hs:1024 + (c + 1) * hs],
            start=True, stop=True), deps=[gc6, egc[-1] if egc else gc4])
        egc.append(K.v(lambda v, c=c: v.tensor_copy(
            sb["expgc"][:, c * 32:c * 32 + hs], sb["p6"][:, 0:hs]), deps=[tb]))
    kr = K.dma("dS", ds["krows"][:, :], k2d[0:96, :], deps=[a_, b_, c_] + dts,
               ch="krows")
    ktf = []
    identf = sb["cmask"][:, 384:512]
    kd_done = [a_, b_, f_, dex, c_]
    for c in range(NCH):
        tt1 = K.t(lambda t, c=c: t.transpose(
            sb["p6"][:, 0:128], k2d[:, c * 128:(c + 1) * 128], identf),
            deps=kd_done + ([ktf[-1]] if ktf else []) + ([egc[-1]] if c == 0 else []))
        e1 = K.v(lambda v, c=c: v.tensor_copy(
            sb["kTf"][:, c * 256:c * 256 + 128], sb["p6"][:, 0:128]), deps=[tt1])
        tt2 = K.t(lambda t, c=c: t.transpose(
            sb["p6"][:, 128:256], k2d2[:, c * 128:(c + 1) * 128], identf),
            deps=kd_done + [e1])
        e2 = K.v(lambda v, c=c: v.tensor_copy(
            sb["kTf"][:, c * 256 + 128:c * 256 + 256], sb["p6"][:, 128:256]),
            deps=[tt2])
        ktf.append(e2)
    return dict(ktf=ktf, egc=egc, krows=[kr])


def emit_pass1(K, sb, dram, ds, g, qk_done, kd):
    H, R, DK, DV = g["H"], g["R"], g["DK"], g["DV"]
    KH = g["KH"]
    XW = DV + 128
    regA, regB = sb["regA"], sb["regB"]
    szb = regB[:, 0:4096]
    ozc = regB[:, 4096:8192]
    ucol = regB[:, 8192:12288]
    kTf, expgc = sb["kTf"], sb["expgc"]
    identf = sb["cmask"][:, 384:512]
    idb = sb["idb"]
    pA = [sb["p0"], sb["p3"]]
    pB = [sb["p1"], sb["p4"]]
    pC = [sb["p2"], sb["p5"]]
    p6 = sb["p6"]
    p7 = sb["p7"]

    z1 = K.v(lambda v: v.memset(szb[:, :], 0.0), deps=qk_done + kd["egc"])
    stok = {h: [z1] for h in range(H)}
    gtok = {h: None for h in range(H)}
    bank = {("A", 0): [z1], ("A", 1): [z1], ("B", 0): [z1], ("B", 1): [z1],
            ("C", 0): [z1], ("C", 1): [z1],
            ("T7", 0): [z1], ("T7", 1): [z1]}
    sbt = {}
    pan_tok = None
    all_t = []
    oz_dma = []
    uc_dma = []
    coll_free = [z1]
    rega_readers = []

    def head_gen(c, ci, h, par, dbg):
        hk = h // R
        cA = kTf[:, c * 256 + 0 + h:c * 256 + 0 + h + 1]
        cB = kTf[:, c * 256 + 32 + h:c * 256 + 32 + h + 1]
        cC = kTf[:, c * 256 + 64 + h:c * 256 + 64 + h + 1]
        cF = kTf[:, c * 256 + 128 + h:c * 256 + 128 + h + 1]
        cE = kTf[:, c * 256 + 160 + h:c * 256 + 160 + h + 1]
        cD = kTf[:, c * 256 + 192 + h:c * 256 + 192 + h + 1]
        egc_c = expgc[:, c * 32 + h:c * 32 + h + 1]
        Qf = regA[:, hk * 256 + ci * 128: hk * 256 + ci * 128 + 128]
        Kf = regA[:, (16 + hk) * 256 + ci * 128: (16 + hk) * 256 + ci * 128 + 128]
        if g["name"] == "a":
            vt = 32 + 2 * h
            Vf = [regA[:, (vt + i) * 256 + ci * 128: (vt + i) * 256 + ci * 128 + 128]
                  for i in range(2)]
        else:
            Vf = [regA[:, (32 + h) * 256 + ci * 128: (32 + h) * 256 + ci * 128 + 128]]
        szb_h = szb[:, h * DV:(h + 1) * DV]
        A_, B_, C_ = pA[par], pB[par], pC[par]
        dexp = sb["regF" if par == 0 else "regG"][:, 0:384]
        bcb = sb["regF" if par == 0 else "regG"][:, 384:768]
        ATd = sb["ATd"][:, par * 128:(par + 1) * 128]
        NTn = sb["NTn"][:, par * 128:(par + 1) * 128]
        Nn = sb["Nn"][:, par * 128:(par + 1) * 128]
        ktok = sb["ktok"][:, par * 128:(par + 1) * 128]
        u0t = sb["u0t"][:, par * 128:(par + 1) * 128]
        qpf = sb["qpf"][:, par * 128:(par + 1) * 128]
        pp = sb["pp"][:, par * 256:(par + 1) * 256]
        ifac = sb["ifacbc"][:, par * 128:(par + 1) * 128]
        xcur = sb["xb"][:, par * 384: par * 384 + XW]
        xnxt = sb["xb2"][:, par * 384: par * 384 + XW]
        sqc = sb["sq"][:, par * 256:(par + 1) * 256]
        sqn = sb["sq2"][:, par * 256:(par + 1) * 256]
        gsl = sb["regC"][:, h * 128:(h + 1) * 128]

        rbC, rbA, rbB = bcb[:, 0:128], bcb[:, 128:256], bcb[:, 256:384]
        rdep = kd["krows"] + sbt.get(("rb", par), [])
        dA = K.dma("dW", rbA,
                   ds["krows"][h:h + 1, c * 128:(c + 1) * 128]
                   .broadcast_to([128, 128]), deps=rdep, ch="rbA%d" % par)
        dB = K.dma("dW", rbB,
                   ds["krows"][32 + h:32 + h + 1, c * 128:(c + 1) * 128]
                   .broadcast_to([128, 128]), deps=rdep, ch="rbB%d" % par)
        dC = K.dma("dW", rbC,
                   ds["krows"][64 + h:64 + h + 1, c * 128:(c + 1) * 128]
                   .broadcast_to([128, 128]), deps=rdep, ch="rbC%d" % par)
        t1 = K.t(lambda t: t.matmul(A_[:, 0:128], Kf, Qf, start=True, stop=True),
                 deps=bank[("A", par)] + [pan_tok])
        t2 = K.t(lambda t: t.matmul(A_[:, 128:256], Kf, Kf, start=True, stop=True),
                 deps=[])
        rega_readers.append(t2)
        K.t(lambda t: t.ldweights(idb[:, :]), deps=[])
        yield
        ddep = sbt.get(("dexp", par), [])
        v4 = K.v(lambda v: v.scalar_tensor_tensor(
            dexp[:, 0:128], rbA, cC, sb["cmask"][:, 0:128],
            ALU.add, ALU.add), deps=[dA] + ddep)
        v5 = K.v(lambda v: v.scalar_tensor_tensor(
            dexp[:, 128:256], rbB, cC, sb["cmask"][:, 128:256],
            ALU.add, ALU.add), deps=[dB])
        v6 = K.v(lambda v: v.scalar_tensor_tensor(
            dexp[:, 256:384], rbC, cB, sb["cmask"][:, 256:384],
            ALU.add, ALU.add), deps=[dC])
        yield
        s1 = K.s(lambda s: s.activation(out=dexp[:, :], in_=dexp[:, :],
                                        func=AF.Exp), deps=[v4, v5, v6])
        s1b = K.s(lambda s: s.activation(out=ifac, in_=rbA,
                                         func=AF.Exp),
                  deps=[dA] + sbt.get(("ifac", par), []))
        sbt[("rb", par)] = [v4, v5, v6, s1b]
        yield
        v7 = K.v(lambda v: v.tensor_tensor(ATd, A_[:, 0:128], dexp[:, 0:128],
                                           ALU.mult),
                 deps=[s1, t2] + sbt.get(("ATd", par), []))
        v8 = K.v(lambda v: v.scalar_tensor_tensor(
            NTn, A_[:, 128:256], -1.0, dexp[:, 128:256], ALU.mult, ALU.mult),
            deps=[s1, t2])
        v9 = K.v(lambda v: v.scalar_tensor_tensor(
            Nn, A_[:, 128:256], -1.0, dexp[:, 256:384], ALU.mult, ALU.mult),
            deps=[s1, t2])
        sbt[("dexp", par)] = [v7, v8, v9]
        bank[("A", par)] = [v7, v8, v9]
        v10 = K.g(lambda v: v.tensor_tensor(qpf, Qf, ifac, ALU.mult),
                  deps=[s1b, pan_tok] + sbt.get(("qpf", par), []))
        sbt[("ifac", par)] = [v10]
        rega_readers.append(v10)
        yield
        if par == 0:
            ktp = p6
            t6 = K.t(lambda t: t.matmul(ktp[:, 0:128], Kf, idb[:, :],
                                        start=True, stop=True),
                     deps=bank[("T7", par)])
        else:
            ktp = p7
            t6 = K.t(lambda t: t.transpose(ktp[:, 0:128], Kf, idb[:, :]),
                     deps=bank[("T7", par)])
        rega_readers.append(t6)
        s2 = K.s(lambda s: s.activation(out=xcur[:, DV:DV + 128],
                                        in_=ktp[:, 0:128], func=AF.Copy,
                                        scale=cF),
                 deps=[t6] + sbt.get(("xb", par), []))
        s3 = K.s(lambda s: s.activation(out=ktok, in_=ktp[:, 0:128],
                                        func=AF.Copy, scale=cD),
                 deps=[t6] + sbt.get(("ktok", par), []))
        yield
        vts = []
        prev_ev = None
        for i, vf in enumerate(Vf):
            if par == 0:
                t7 = K.t(lambda t, vf=vf: t.matmul(
                    ktp[:, 128:256], vf, idb[:, :], start=True, stop=True),
                    deps=([s2, s3] if i == 0 else []) +
                         ([prev_ev] if prev_ev else []))
            else:
                t7 = K.t(lambda t, vf=vf: t.transpose(ktp[:, 128:256], vf,
                                                      idb[:, :]),
                         deps=([s2, s3] if i == 0 else []) +
                              ([prev_ev] if prev_ev else []))
            rega_readers.append(t7)
            ev = K.v(lambda v, i=i: v.tensor_copy(
                xcur[:, i * 128:(i + 1) * 128], ktp[:, 128:256]), deps=[t7])
            ec = K.v(lambda v, i=i: v.tensor_scalar(
                xcur[:, i * 128:(i + 1) * 128], xcur[:, i * 128:(i + 1) * 128],
                cE, None, ALU.mult), deps=[ev])
            prev_ev = ec
            vts.append(ec)
        bank[("T7", par)] = vts + [s2, s3]
        yield
        lhsT_T, lhsT_A = NTn, Nn
        acur = [v8, v9] + vts
        xtok = vts + [s2]
        for lev in range(7):
            tx = K.t(lambda t, lhsT_T=lhsT_T, xc=xcur: t.matmul(
                A_[:, 0:XW], lhsT_T[:, :], xc[:, 0:XW], start=True, stop=True),
                deps=acur + xtok + bank[("A", par)])
            ex = K.v(lambda v, xn=xnxt, xc=xcur: v.tensor_tensor(
                xn[:, 0:XW], A_[:, 0:XW], xc[:, 0:XW], ALU.add), deps=[tx])
            bank[("A", par)] = [ex]
            xtok = [ex]
            xcur, xnxt = xnxt, xcur
            if lev % 2 == 1:
                K.t(lambda t: t.ldweights(idb[:, :]), deps=[])
            if lev < 6:
                ta = K.t(lambda t, lhsT_T=lhsT_T, lhsT_A=lhsT_A: t.matmul(
                    B_[:, 0:128], lhsT_T[:, :], lhsT_A[:, :],
                    start=True, stop=True), deps=acur + bank[("B", par)])
                tb = K.t(lambda t, lhsT_T=lhsT_T, lhsT_A=lhsT_A: t.matmul(
                    B_[:, 128:256], lhsT_A[:, :], lhsT_T[:, :],
                    start=True, stop=True), deps=[])
                if lev % 2 == 0:
                    ea = K.v(lambda v, sq=sqn: v.tensor_copy(sq[:, 0:256],
                                                             B_[:, 0:256]),
                             deps=[ta, tb])
                else:
                    ea = K.s(lambda s, sq=sqn: s.activation(
                        out=sq[:, 0:256], in_=B_[:, 0:256], func=AF.Copy),
                        deps=[ta, tb])
                bank[("B", par)] = [ea]
                acur = [ea]
                lhsT_A = sqn[:, 0:128]
                lhsT_T = sqn[:, 128:256]
                sqc, sqn = sqn, sqc
            yield
        Xf = xcur
        xft = xtok[0]
        sbt[("xb", par)] = xtok
        t9 = K.t(lambda t: t.matmul(C_[:, 0:128], Xf[:, DV:DV + 128], ATd,
                                    start=True, stop=False),
                 deps=[xft, v7] + bank[("C", par)])
        t10 = K.t(lambda t: t.matmul(C_[:, 0:128], idb[:, :], qpf,
                                     start=False, stop=True), deps=[v10])
        e_u0 = K.v(lambda v: v.tensor_copy(u0t, C_[:, 0:128]),
                   deps=[t10] + sbt.get(("u0t", par), []))
        yield
        if c == 0:
            t11 = K.t(lambda t: t.matmul(C_[:, 128:256], idb[:, :], u0t,
                                         start=True, stop=True), deps=[e_u0])
        else:
            t11 = K.t(lambda t: t.matmul(C_[:, 128:256], gsl, u0t,
                                         start=True, stop=True),
                      deps=[e_u0, gtok[h]])
        e_uc = K.s(lambda s: s.activation(out=ucol[:, h * 128:(h + 1) * 128],
                                          in_=C_[:, 128:256], func=AF.Copy),
                   deps=[t11] + coll_free)
        yield
        t12 = K.t(lambda t: t.matmul(C_[:, 256:384], ktok, Xf[:, DV:DV + 128],
                                     start=True, stop=True),
                  deps=[s3, xft, e_uc])
        t13 = K.t(lambda t: t.matmul(C_[:, 384:512], Xf[:, DV:DV + 128], ktok,
                                     start=True, stop=True), deps=[])
        e_p = K.v(lambda v: v.scalar_tensor_tensor(
            pp[:, 0:128], identf, egc_c, C_[:, 256:384], ALU.mult, ALU.add),
            deps=[t12, t13] + sbt.get(("pp", par), []))
        e_pt = K.v(lambda v: v.scalar_tensor_tensor(
            pp[:, 128:256], identf, egc_c, C_[:, 384:512], ALU.mult, ALU.add),
            deps=[t13])
        sbt[("u0t", par)] = [t11, t12, t13]
        yield
        if c < NCH - 1:
            if c == 0:
                t14 = K.t(lambda t: t.matmul(B_[:, 256:384], pp[:, 128:256],
                                             idb[:, :], start=True, stop=True),
                          deps=[e_pt, bank[("B", par)][0]])
            else:
                t14 = K.t(lambda t: t.matmul(B_[:, 256:384], pp[:, 128:256],
                                             gsl, start=True, stop=True),
                          deps=[e_pt, gtok[h], bank[("B", par)][0]])
            ng = K.v(lambda v: v.tensor_copy(gsl, B_[:, 256:384]),
                     deps=[t14, t11])
            gtok[h] = ng
            bank[("B", par)] = bank[("B", par)] + [ng]
        yield
        cho = 256 if DV == 256 else 128
        to1 = K.t(lambda t: t.matmul(C_[:, 0:DV], ATd, Xf[:, 0:DV],
                                     start=True, stop=False),
                  deps=[xft, v7, e_u0, e_uc, e_p, e_pt])
        to2 = K.t(lambda t: t.matmul(C_[:, 0:DV], u0t, szb_h,
                                     start=False, stop=True),
                  deps=[e_u0] + stok[h])
        e_oz = K.s(lambda s: s.activation(out=ozc[:, h * DV:(h + 1) * DV],
                                          in_=C_[:, 0:DV], func=AF.Copy),
                   deps=[to2] + coll_free)
        K.t(lambda t: t.ldweights(idb[:, :]), deps=[])
        yield
        tc1 = K.t(lambda t: t.matmul(C_[:, cho:cho + DV], ktok, Xf[:, 0:DV],
                                     start=True, stop=False),
                  deps=[s3, xft, e_oz])
        tc2 = K.t(lambda t: t.matmul(C_[:, cho:cho + DV], pp[:, 128:256], szb_h,
                                     start=False, stop=True),
                  deps=[e_pt] + stok[h])
        e_s2 = K.v(lambda v: v.tensor_copy(szb_h, C_[:, cho:cho + DV]),
                   deps=[tc2])
        stok[h] = [e_s2]
        bank[("C", par)] = [e_oz, e_s2]
        sbt[("ktok", par)] = [tc1]
        sbt[("qpf", par)] = [t10]
        sbt[("ATd", par)] = [to1]
        all_t.append(e_s2)
        return

    for pan in range(4):
        src_ = ds["qkvpost"][:, pan * 256:(pan + 1) * 256].rearrange(
            "(t p) c -> p t c", p=128)
        pan_dep = qk_done + ([pan_tok] if pan_tok else []) + kd["ktf"] + \
            rega_readers
        rega_readers = []
        pan_tok = K.dma("dS", regA.rearrange("p (t c) -> p t c", t=64), src_,
                        deps=[d for d in pan_dep if d], ch="pcv")
        for ci in range(2):
            c = pan * 2 + ci
            gens = []
            order = list(range(H))
            head_done = []
            gi = 0
            active = []
            while active or gi < len(order):
                while len(active) < 2 and gi < len(order):
                    h = order[gi]
                    active.append(head_gen(c, ci, h, gi % 2, None))
                    gi += 1
                done_idx = []
                for k_, gen_ in enumerate(active):
                    try:
                        next(gen_)
                    except StopIteration:
                        done_idx.append(k_)
                for k_ in reversed(done_idx):
                    active.pop(k_)
            head_done = [stok[h][0] for h in range(H)] + \
                [Tok(K.st["scalar"]["sem"], K.st["scalar"]["cnt"]),
                 Tok(K.st["vector"]["sem"], K.st["vector"]["cnt"])]
            oz_dma.append(K.dma("dS", ds["ozero"][c, :, 0:H * DV],
                                ozc[:, 0:H * DV], deps=head_done, ch="ozst"))
            uc_dma.append(K.dma("dS", ds["util"][c, :, 0:H * 128],
                                ucol[:, 0:H * 128], deps=head_done, ch="ucst"))
            coll_free = [oz_dma[-1], uc_dma[-1]]
    chain = [stok[h][0] for h in range(H)]
    return dict(chain=chain + oz_dma + uc_dma, all=all_t + oz_dma + uc_dma,
                oz=oz_dma, uc=uc_dma)


def emit_pass2(K, sb, dram, ds, g, sin_tok, p1):
    H, R, DK, DV = g["H"], g["R"], g["DK"], g["DV"]
    regA, regB = sb["regA"], sb["regB"]
    zc = regB[:, 0:4096]
    ozc = regB[:, 4096:8192]
    ucol = regB[:, 8192:12288]
    otrb = regB[:, 12288:16384]
    sinb = sb["regC"]
    gfc = regA[:, 0:4096]
    normc = sb["normc"]
    idb = sb["idb"]
    p0, p1b = sb["p0"], sb["p7"]
    ld_free = [sin_tok]
    gf_dma = []
    for c in range(NCH):
        l1 = K.dma("dS", ozc[:, 0:H * DV], ds["ozero"][c, :, 0:H * DV],
                   deps=p1["oz"] + ld_free, ch="oz2")
        l2 = K.dma("dS", ucol[:, 0:H * 128], ds["util"][c, :, 0:H * 128],
                   deps=p1["uc"] + ld_free, ch="uc2")
        lz = None
        if g["name"] == "b":
            lz = K.dma("dS", zc[:, :], ds["zraw"][c, :, :],
                       deps=ld_free, ch="z2")
        w1 = []
        bank0 = [sin_tok]
        for h in range(H):
            tcr = K.t(lambda t, h=h: t.matmul(
                p0[:, 0:DV], ucol[:, h * 128:(h + 1) * 128],
                sinb[:, h * DV:(h + 1) * DV], start=True, stop=True),
                deps=[l2, sin_tok] + bank0)
            eo = K.v(lambda v, h=h: v.tensor_tensor(
                otrb[:, h * DV:(h + 1) * DV], p0[:, 0:DV],
                ozc[:, h * DV:(h + 1) * DV], ALU.add), deps=[tcr, l1])
            bank0 = [eo]
            if g["name"] == "b":
                sqs = sb["sqscr"][:, (h % 2) * 128:(h % 2) * 128 + 128]
                w1.append(K.s(lambda s, h=h, sqs=sqs: s.activation(
                    out=sqs, in_=otrb[:, h * DV:(h + 1) * DV],
                    func=AF.Square, accum_out=normc[:, h:h + 1]),
                    deps=[eo, w1[-2] if len(w1) >= 2 else None]))
            else:
                w1.append(eo)
        if g["name"] == "b":
            r0 = K.v(lambda v: v.tensor_scalar(normc[:, 0:32], normc[:, 0:32],
                                               1.0 / 128.0, 1e-6, ALU.mult,
                                               ALU.add), deps=w1)
            r1 = K.s(lambda s: s.activation(out=normc[:, 0:32],
                                            in_=normc[:, 0:32], func=AF.Ln,
                                            scale=1.0,
                                            bias=sb["consts"][:, 4:5]), deps=[r0])
            r1b = K.v(lambda v: v.tensor_scalar(normc[:, 0:32], normc[:, 0:32],
                                                -0.5, None, ALU.mult), deps=[r1])
            r2 = K.s(lambda s: s.activation(out=normc[:, 0:32],
                                            in_=normc[:, 0:32], func=AF.Exp),
                     deps=[r1b])
        w2 = []
        bank1 = [gf_dma[-1]] if gf_dma else []
        for h in range(H):
            if g["name"] == "b":
                on = K.v(lambda v, h=h: v.tensor_scalar(
                    sb["otn"][:, 0:128], otrb[:, h * 128:(h + 1) * 128],
                    normc[:, h:h + 1], None, ALU.mult), deps=[r2] + w1)
                tt = K.t(lambda t: t.transpose(p1b[:, 0:128], sb["otn"][:, 0:128],
                                               idb[:, :]), deps=[on] + bank1)
                zs = K.s(lambda s, h=h: s.activation(
                    out=sb["zsil"][:, :], in_=zc[:, h * 128:(h + 1) * 128],
                    func=AF.Silu), deps=[lz, w2[-1] if w2 else None])
                gf = K.v(lambda v, h=h: v.scalar_tensor_tensor(
                    gfc[:, h * 128:(h + 1) * 128], p1b[:, 0:128],
                    sb["bsmall"][:, 2:3], sb["zsil"][:, :], ALU.mult, ALU.mult),
                    deps=[tt, zs])
                bank1 = [gf]
                w2.append(gf)
            else:
                for i in range(2):
                    tt = K.t(lambda t, h=h, i=i: t.transpose(
                        p1b[:, 0:128], otrb[:, h * 256 + i * 128:
                                            h * 256 + (i + 1) * 128], idb[:, :]),
                        deps=w1 + bank1)
                    gf = K.v(lambda v, h=h, i=i: v.tensor_copy(
                        gfc[:, (2 * h + i) * 128:(2 * h + i + 1) * 128],
                        p1b[:, 0:128]), deps=[tt])
                    bank1 = [gf]
                    w2.append(gf)
            K.t(lambda t: t.ldweights(idb[:, :]), deps=[])
        gf_dma.append(K.dma(
            "dS", ds["ofin"][:, :, c * 128:(c + 1) * 128].rearrange(
                "t p j -> p t j"),
            gfc.rearrange("p (t j) -> p t j", j=128), deps=w2, ch="gf"))
        ld_free = [gf_dma[-1]] + w2
    owb = sb["regD"][:, 0:2048]
    orhs = sb["regD"][:, 2048:3072]
    res = []
    prev_w = [None, None, None, None]
    prev_r = [None, None]
    bankf = {}
    psl = [sb["p%d" % i] for i in range(4)]
    for mg in range(4):
        for half in range(2):
            last_mm = None
            for k in range(32):
                wsl = k % 4
                rsl = k % 2
                wt = K.dma("dW", owb[:, wsl * 512:(wsl + 1) * 512],
                           dram[g["WO"]][k * 128:(k + 1) * 128,
                                         mg * 512:(mg + 1) * 512],
                           deps=[prev_w[wsl]], ch="ow%d" % wsl)
                rt = K.dma("dS", orhs[:, rsl * 512:(rsl + 1) * 512],
                           ds["ofin"][k, :, half * 512:(half + 1) * 512],
                           deps=[prev_r[rsl], gf_dma[-1]], ch="or%d" % rsl)
                for m in range(4):
                    extra = [bankf.get(m)] if k == 0 else []
                    last_mm = K.t(lambda t, wsl=wsl, rsl=rsl, m=m, k=k:
                                  t.matmul(psl[m][:, :],
                                           owb[:, wsl * 512 + m * 128:
                                               wsl * 512 + (m + 1) * 128],
                                           orhs[:, rsl * 512:(rsl + 1) * 512],
                                           start=(k == 0), stop=(k == 31)),
                                  deps=[wt, rt] + [e for e in extra if e])
                prev_w[wsl] = last_mm
                prev_r[rsl] = last_mm
            for m in range(4):
                dm = mg * 4 + m
                xs = sb["xT"][:, dm * T + half * 512: dm * T + (half + 1) * 512]
                e = K.v(lambda v, m=m, xs=xs: v.tensor_tensor(
                    xs, psl[m][:, :], xs, ALU.add), deps=[last_mm])
                bankf[m] = e
                res.append(e)
    return res


def emit_store(K, sb, out, deps):
    toks = []
    for ft in range(NFT):
        toks.append(K.dma("dS", out[ft * 128:(ft + 1) * 128, :],
                          sb["xT"][:, ft * T:(ft + 1) * T], deps=deps,
                          ch="out%d" % (ft % 4)))
    allch = [Tok(c["sem"], c["cnt"]) for c in K.dma_ch.values()]
    K.waiter("sync", allch + toks)
    return toks


def emit_all(K, sb, dram, ds, out):
    dep = emit_init(K, sb, dram)
    if EMIT_ATTN_A:
        dep = emit_mixer(K, sb, dram, ds, dep, GEO_A)
    if EMIT_FFN:
        dep = emit_ffn(K, sb, dram, ds, dep)
    if EMIT_ATTN_B:
        dep = emit_mixer(K, sb, dram, ds, dep, GEO_B)
    if EMIT_FFN:
        dep = emit_ffn(K, sb, dram, ds, dep)
    emit_store(K, sb, out, dep)



_CACHE = {}


def kernel(**inputs):
    per_core = host_prep(inputs)
    if "nc" not in _CACHE:
        _CACHE["nc"] = build_program()
    nc = _CACHE["nc"]
    res = run_bass_kernel_spmd(nc, per_core, core_ids=list(range(8)))
    x = np.asarray(inputs["L0_attn_a_x"])
    out = np.empty_like(x, dtype=np.float32)
    for core in range(8):
        b, h = core // 2, core % 2
        out[b, h * T:(h + 1) * T, :] = res.results[core]["out"].T
    return out
